# version 18
# baseline (speedup 1.0000x reference)
import numpy as np
from contextlib import ExitStack

import concourse.bass as bass
import concourse.mybir as mybir
from concourse.bass_utils import run_bass_kernel_spmd

F32 = mybir.dt.float32
BF16 = mybir.dt.bfloat16
AF = mybir.ActivationFunctionType
ALU = mybir.AluOpType
AX = mybir.AxisListType

L, DM, H, HD, LD, NE = 2048, 1024, 16, 64, 256, 8
NT = L // 128
NFC = DM // 128
LNEPS = 1e-5

ENGINES = ("tensor", "vector", "scalar", "sync", "gpsimd")
DMA_ENGINES = ("sync", "gpsimd")
NDMASEM = 4
SEMKEYS = ["tensor", "vector", "scalar"] + [
    f"{e}{i}" for e in DMA_ENGINES for i in range(NDMASEM)]


class Buf:
    __slots__ = ("name", "w", "r")

    def __init__(self, name):
        self.name = name
        self.w = {}
        self.r = {}


class Sched:
    def __init__(self):
        self.prog = {e: [] for e in ENGINES}
        self.count = {k: 0 for k in SEMKEYS}
        self.bound = {e: {k: 0 for k in SEMKEYS} for e in ENGINES}
        self.dma_rr = {e: 0 for e in DMA_ENGINES}
        self.sems = {}

    def op(self, eng, build, reads=(), writes=()):
        deps = {}
        for b in reads:
            for we, wv in b.w.items():
                deps[we] = max(deps.get(we, 0), wv)
        for b in writes:
            for we, wv in b.w.items():
                deps[we] = max(deps.get(we, 0), wv)
            for re, rv in b.r.items():
                deps[re] = max(deps.get(re, 0), rv)
        if eng in DMA_ENGINES:
            key = f"{eng}{self.dma_rr[eng] % NDMASEM}"
            self.dma_rr[eng] += 1
            step = 16
            deps[key] = max(deps.get(key, 0), self.count[key])
        else:
            key, step = eng, 1
        waits = []
        for de, dv in deps.items():
            if self.bound[eng][de] >= dv:
                continue
            waits.append((de, dv))
            self.bound[eng][de] = dv
        self.count[key] += step
        v = self.count[key]
        sems = self.sems

        def thunk(obj, _waits=tuple(waits), _build=build, _key=key,
                  _step=step):
            for de, dv in _waits:
                obj.wait_ge(sems[de], dv)
            last = _build(obj)
            last.then_inc(sems[_key], _step)

        self.prog[eng].append(thunk)
        for b in writes:
            b.w = {key: v}
            b.r = {}
        for b in reads:
            b.r[key] = max(b.r.get(key, 0), v)
        return v

    def raw(self, eng, build):
        self.prog[eng].append(lambda obj, _b=build: _b(obj))


def build_nc():
    nc = bass.Bass("TRN2", target_bir_lowering=False)
    ap = lambda t: t.ap()
    d_x = ap(nc.dram_tensor("x", [L, DM], F32, kind="ExternalInput"))
    d_wq = ap(nc.dram_tensor("wq", [128, 8192], F32, kind="ExternalInput"))
    d_wkv = ap(nc.dram_tensor("wkv", [128, 2048], F32, kind="ExternalInput"))
    d_wku = ap(nc.dram_tensor("wku", [128, 2048], F32, kind="ExternalInput"))
    d_wvu = ap(nc.dram_tensor("wvu", [128, 2048], F32, kind="ExternalInput"))
    d_wout = ap(nc.dram_tensor("wout", [128, 8192], F32, kind="ExternalInput"))
    d_wg = ap(nc.dram_tensor("wg", [128, 64], F32, kind="ExternalInput"))
    d_bg = ap(nc.dram_tensor("bgc", [NE, 1], F32, kind="ExternalInput"))
    d_we = ap(nc.dram_tensor("we", [128, 8192], F32, kind="ExternalInput"))
    d_bet = ap(nc.dram_tensor("bet", [128, NFC], F32, kind="ExternalInput"))
    d_id = ap(nc.dram_tensor("cid", [128, 128], F32, kind="ExternalInput"))
    d_one = ap(nc.dram_tensor("cones", [128, 128], F32, kind="ExternalInput"))
    d_tri = ap(nc.dram_tensor("ctri", [128, 128], F32, kind="ExternalInput"))
    d_seg = ap(nc.dram_tensor("cseg", [128, 2], F32, kind="ExternalInput"))
    d_segs = ap(nc.dram_tensor("csegs", [128, 128], BF16,
                               kind="ExternalInput"))
    d_segb = ap(nc.dram_tensor("csegb", [128, 2], BF16,
                               kind="ExternalInput"))
    d_y = ap(nc.dram_tensor("y", [L, DM], F32, kind="ExternalOutput"))
    d_wkvh = ap(nc.dram_tensor("wkvh", [128, 2048], BF16, kind="Internal"))
    d_wkvl = ap(nc.dram_tensor("wkvl", [128, 2048], BF16, kind="Internal"))
    d_wqh = ap(nc.dram_tensor("wqh", [128, 8192], BF16, kind="Internal"))
    d_wql = ap(nc.dram_tensor("wql", [128, 8192], BF16, kind="Internal"))

    es = ExitStack()
    sb = lambda name, fr: es.enter_context(
        nc.sbuf_tensor(name, [128, fr], F32))
    A = sb("A", 16384)
    B = sb("B", 16384)
    C = sb("C", 4096)
    KP = sb("KP", 2048)
    PS1 = sb("PS1", 2048)
    VB = sb("VB", 2080)
    EB = sb("EB", 2048)
    WS = sb("WS", 2048)
    WU = sb("WU", 1024)
    XT = sb("XT", 2048)
    SCR = sb("SCR", 2048)
    IDT = sb("IDT", 128)
    ONE = sb("ONE", 128)
    TRI = sb("TRI", 128)
    SEG = sb("SEG", 2)
    SEGS = es.enter_context(nc.sbuf_tensor("SEGS", [128, 128], BF16))
    SEGB = es.enter_context(nc.sbuf_tensor("SEGB", [128, 2], BF16))
    BE = sb("BE", NFC)
    BG = sb("BG", 1)
    ROWS = sb("ROWS", 32)
    NEGB = sb("NEGB", 16)
    MV = sb("MV", 16)
    EPSC = sb("EPSC", 1)
    NS = sb("NS", 48)
    LGT = sb("LGT", 128)
    GES = sb("GES", 128)
    GSS = sb("GSS", 32)
    QN = KNS = DRS = LG = BPROW = SCR

    ps = [es.enter_context(nc.psum_tensor(f"p{i}", [128, 512], F32))
          for i in range(8)]

    PS1B = nc.alloc_sbuf_tensor_at("PS1B", [128, 4096], BF16,
                                   offset=nc.lookup_mloc(PS1).addr)
    KB = nc.alloc_sbuf_tensor_at("KB", [128, 4096], BF16,
                                 offset=nc.lookup_mloc(KP).addr)
    AB = nc.alloc_sbuf_tensor_at("AB", [128, 32768], BF16,
                                 offset=nc.lookup_mloc(A).addr)
    WSB = nc.alloc_sbuf_tensor_at("WSB", [128, 4096], BF16,
                                  offset=nc.lookup_mloc(WS).addr)
    BB = nc.alloc_sbuf_tensor_at("BB", [128, 32768], BF16,
                                 offset=nc.lookup_mloc(B).addr)
    EBB = nc.alloc_sbuf_tensor_at("EBB", [128, 4096], BF16,
                                  offset=nc.lookup_mloc(EB).addr)
    s = Sched()
    for k in SEMKEYS:
        s.sems[k] = es.enter_context(nc.semaphore(f"sem_{k}"))

    bC, bKP, bVB = (Buf(n) for n in "C KP VB".split())
    bPS1h = [Buf("PS1a"), Buf("PS1b")]
    bPS1 = bPS1h
    bAs = [Buf(f"A{t}") for t in range(NT)]
    bBs = [Buf(f"B{t}") for t in range(NT)]
    bEB = [Buf(f"EB{i}") for i in range(4)]
    bWS = [Buf("WS0"), Buf("WS1")]
    bWU = [Buf("WU0"), Buf("WU1")]
    bXT = [Buf("XT0"), Buf("XT1")]
    bP = [Buf(f"p{i}") for i in range(8)]
    bCONST = Buf("CONST")
    bSCR = Buf("SCR")
    bQN = bKNS = bDRS = bLG = bBPROW = bSCR
    bROWS, bNEGB = Buf("ROWS"), Buf("NEGB")
    bMV = [Buf(f"MV{i}") for i in range(4)]
    bNS = [Buf(f"NS{i}") for i in range(4)]
    bLGT, bGES, bGSS, bDSCR = (Buf(n) for n in "LGT GES GSS DSCR".split())
    bY = Buf("Y")
    bWH = Buf("WHD")

    def dma(out, in_):
        return lambda o: o.dma_start(out=out, in_=in_)

    s.op("sync", dma(IDT[:], d_id[:, :]), writes=[bCONST])
    s.op("sync", dma(ONE[:], d_one[:, :]), writes=[bCONST])
    s.op("sync", dma(TRI[:], d_tri[:, :]), writes=[bCONST])
    s.op("sync", dma(SEG[:], d_seg[:, :]), writes=[bCONST])
    s.op("sync", dma(SEGS[:], d_segs[:, :]), writes=[bCONST])
    s.op("sync", dma(SEGB[:], d_segb[:, :]), writes=[bCONST])
    s.op("vector", lambda o: o.memset(EPSC[:], LNEPS), writes=[bCONST])
    s.op("sync", dma(BE[:], d_bet[:, :]), writes=[bCONST])
    s.op("vector", lambda o: o.memset(VB[:], 0.0), writes=[bVB])
    vb3 = VB.ap().rearrange("p (s c) -> p s c", c=130)
    s.op("vector", lambda o: o.memset(vb3[:, :, 64:65], 1.0), writes=[bVB])
    s.op("vector", lambda o: o.memset(vb3[:, :, 129:130], 1.0), writes=[bVB])
    s.op("sync", dma(BG[0:NE, :], d_bg[:, :]), writes=[bCONST])

    def wsplit(dsrc, dhid, dlod, m, land, landbuf):
        src = dsrc[:, m * 1024:(m + 1) * 1024]
        s.op("sync", dma(land, src), writes=[landbuf])
        slot = m % 2
        hi = WSB[:, slot * 2048:slot * 2048 + 1024]
        lo = WSB[:, slot * 2048 + 1024:slot * 2048 + 2048]
        s.op("scalar", lambda o, hi=hi, land=land: o.copy(hi, land),
             reads=[landbuf], writes=[bWS[slot]])
        s.op("vector", lambda o, hi=hi, lo=lo, land=land: o.tensor_sub(
            lo, land, hi), reads=[landbuf, bWS[slot]], writes=[bWS[slot]])
        s.op("gpsimd", dma(dhid[:, m * 1024:(m + 1) * 1024], hi),
             reads=[bWS[slot]], writes=[bWH])
        s.op("gpsimd", dma(dlod[:, m * 1024:(m + 1) * 1024], lo),
             reads=[bWS[slot]], writes=[bWH])

    def stream_b16(dhid, dlod, m, slot):
        s.op("sync", dma(WSB[:, slot * 2048:slot * 2048 + 1024],
                         dhid[:, m * 1024:(m + 1) * 1024]),
             reads=[bWH], writes=[bWS[slot]])
        s.op("sync", dma(WSB[:, slot * 2048 + 1024:slot * 2048 + 2048],
                         dlod[:, m * 1024:(m + 1) * 1024]),
             reads=[bWH], writes=[bWS[slot]])

    def proj3(wslot, kchunks, bank, rh_of, rl_of, rbufs, evac):
        def mmg(o, wslot=wslot, kchunks=kchunks, bank=bank):
            for kc in range(kchunks):
                hw_ = WSB[:, wslot * 2048 + kc * 128:
                          wslot * 2048 + kc * 128 + 128]
                lw_ = WSB[:, wslot * 2048 + 1024 + kc * 128:
                          wslot * 2048 + 1024 + kc * 128 + 128]
                r = o.matmul(ps[bank][:], hw_, rh_of(kc), start=(kc == 0),
                             stop=False)
                r = o.matmul(ps[bank][:], hw_, rl_of(kc), start=False,
                             stop=False)
                r = o.matmul(ps[bank][:], lw_, rh_of(kc), start=False,
                             stop=(kc == kchunks - 1))
            return r
        s.op("tensor", mmg, reads=[bWS[wslot]] + rbufs, writes=[bP[bank]])
        evac(bank)

    for m in range(2):
        wsplit(d_wkv, d_wkvh, d_wkvl, m,
               C[:, (m % 2) * 1024:(m % 2) * 1024 + 1024], bC)
    for m in range(8):
        wsplit(d_wq, d_wqh, d_wql, m,
               C[:, (m % 2) * 1024:(m % 2) * 1024 + 1024], bC)

    def ln_stats(xbuf, xslice, par=0):
        mo, no = par * 4, par * 12
        s.op("vector", lambda o: (
            o.bn_stats(NS[:, no:no + 6], xslice[:, 0:512]),
            o.bn_stats(NS[:, no + 6:no + 12], xslice[:, 512:1024]))[-1],
            reads=[xbuf], writes=[bNS[par]])
        s.op("vector", lambda o: o.bn_aggr(MV[:, mo:mo + 2],
                                           NS[:, no:no + 12]),
             reads=[bNS[par]], writes=[bMV[par]])
        s.op("scalar", lambda o: o.activation(MV[:, mo + 3:mo + 4],
                                              MV[:, mo + 1:mo + 2], AF.Ln,
                                              bias=EPSC[:, 0:1], scale=1.0),
             reads=[bMV[par], bCONST], writes=[bMV[par]])
        s.op("scalar", lambda o: o.activation(MV[:, mo + 2:mo + 3],
                                              MV[:, mo + 3:mo + 4], AF.Exp,
                                              scale=-0.5),
             reads=[bMV[par]], writes=[bMV[par]])

    def tr_tile_to(dstT, t, srcbuf, srctensor, banks, dstbuf, srcoff=0,
                   dstbuf_is_list=False, dstTlo=None):
        src = srctensor if srctensor is not None else PS1
        dbufs = dstbuf if dstbuf_is_list else [dstbuf]
        for hf in range(2):
            bank = banks[hf]

            def trf(o, hf=hf, bank=bank):
                for j in range(4):
                    r = o.transpose(
                        ps[bank][:, j * 128:(j + 1) * 128],
                        src[:, srcoff + (hf * 4 + j) * 128:
                            srcoff + (hf * 4 + j + 1) * 128],
                        IDT[:])
                return r
            s.op("tensor", trf, reads=[srcbuf, bCONST], writes=[bP[bank]])

            def evf(o, hf=hf, bank=bank, t=t):
                for j in range(4):
                    f = hf * 4 + j
                    r = o.copy(dstT(f, t), ps[bank][:, j * 128:(j + 1) * 128])
                return r
            s.op("scalar", evf, reads=[bP[bank]], writes=dbufs)
            if dstTlo is not None:
                def evl(o, hf=hf, bank=bank, t=t):
                    for j in range(4):
                        f = hf * 4 + j
                        r = o.tensor_sub(dstTlo(f, t),
                                         ps[bank][:, j * 128:(j + 1) * 128],
                                         dstT(f, t))
                    return r
                s.op("vector", evl, reads=[bP[bank]] + dbufs, writes=dbufs)

    def p1_a(t):
        par = t % 2
        xs_ = XT[:, par * 1024:par * 1024 + 1024]
        s.op("gpsimd", dma(xs_, d_x[t * 128:(t + 1) * 128, :]),
             writes=[bXT[par]])
        ln_stats(bXT[par], xs_, par)

    def p1_b(t):
        par = t % 2
        xs_ = XT[:, par * 1024:par * 1024 + 1024]
        wk_ = PS1[:, par * 1024:par * 1024 + 1024]
        mo = par * 4
        s.op("vector", lambda o, xs_=xs_, wk_=wk_, mo=mo: o.tensor_scalar(
            wk_, xs_, MV[:, mo:mo + 1], MV[:, mo + 2:mo + 3],
            ALU.subtract, ALU.mult),
            reads=[bXT[par], bMV[par]], writes=[bPS1h[par]])
        tr_tile_to(lambda f, tt: AB[:, 2 * (f * 2048 + tt * 128):
                                    2 * (f * 2048 + tt * 128) + 128],
                   t, bPS1h[par], PS1, (par * 2, par * 2 + 1),
                   [bAs[t]], srcoff=par * 1024, dstbuf_is_list=True,
                   dstTlo=lambda f, tt: AB[:, 2 * (f * 2048 + tt * 128) + 128:
                                           2 * (f * 2048 + tt * 128) + 256])

    def proj_group(dst, dstbufs, wslot, rhsbufs, rhs_of, kchunks, nb, bank,
                   scale=None, evac=None):
        def mmg(o, nb=nb, bank=bank):
            for kc in range(kchunks):
                r = o.matmul(
                    ps[bank][:],
                    WS[:, wslot * 1024 + kc * 128:
                       wslot * 1024 + kc * 128 + 128],
                    rhs_of(kc, nb),
                    start=(kc == 0), stop=(kc == kchunks - 1))
            return r
        s.op("tensor", mmg, reads=[bWS[wslot]] + rhsbufs(nb),
             writes=[bP[bank]])
        if evac is not None:
            evac(bank)
        elif scale is None:
            s.op("scalar", lambda o, nb=nb, bank=bank: o.copy(
                dst(nb), ps[bank][:]), reads=[bP[bank]],
                writes=dstbufs(nb))
        else:
            s.op("scalar", lambda o, nb=nb, bank=bank: o.activation(
                dst(nb), ps[bank][:], AF.Copy, scale=scale),
                reads=[bP[bank]], writes=dstbufs(nb))

    def proj_mblock(dst, dstbufs, wslot, rhsbufs, rhs_of, kchunks, scale=None,
                    banks=(0, 1, 2, 3)):
        for nb in range(4):
            proj_group(dst, dstbufs, wslot, rhsbufs, rhs_of, kchunks, nb,
                       banks[nb % len(banks)], scale)

    lnT_of = lambda kc, nb: A[:, kc * 2048 + nb * 512:
                              kc * 2048 + nb * 512 + 512]

    def p2_part(nb):
        def lnh(kc, nb=nb):
            v = AB[:, 2 * (kc * 2048 + nb * 512):
                   2 * (kc * 2048 + nb * 512) + 1024]
            return v.rearrange("p (t u c) -> p t u c", u=2, c=128)[
                :, :, 0:1, :]

        def lnl(kc, nb=nb):
            v = AB[:, 2 * (kc * 2048 + nb * 512):
                   2 * (kc * 2048 + nb * 512) + 1024]
            return v.rearrange("p (t u c) -> p t u c", u=2, c=128)[
                :, :, 1:2, :]

        for m in range(2):
            slot = m % 2
            stream_b16(d_wkvh, d_wkvl, m, slot)
            bank = 4 + m % 4

            def evc(bank, m=m, nb=nb):
                s.op("scalar", lambda o, bank=bank, m=m, nb=nb: o.copy(
                    C[:, m * 2048 + nb * 512:m * 2048 + nb * 512 + 512],
                    ps[bank][:]), reads=[bP[bank]], writes=[bC])
            proj3(slot, 8, bank, lnh, lnl, bAs[4 * nb:4 * nb + 4], evc)
        for m in range(8):
            slot = m % 2
            stream_b16(d_wqh, d_wql, m, slot)
            bank = 4 + (m + 2) % 4

            def evq(bank, m=m, nb=nb):
                r2 = 2 * (m * 2048 + nb * 512)
                qh = BB[:, r2:r2 + 512]
                ql = BB[:, r2 + 512:r2 + 1024]
                dd = bBs[4 * nb:4 * nb + 4]
                s.op("scalar", lambda o, bank=bank, qh=qh: o.activation(
                    qh, ps[bank][:], AF.Copy, scale=0.125),
                    reads=[bP[bank]], writes=dd)
                s.op("vector", lambda o, bank=bank, qh=qh, ql=ql:
                     o.scalar_tensor_tensor(ql, ps[bank][:], 0.125, qh,
                                            ALU.mult, ALU.subtract),
                     reads=[bP[bank]] + dd, writes=dd)
            proj3(slot, 8, bank, lnh, lnl, bAs[4 * nb:4 * nb + 4], evq)

    p1_a(0)
    for t in range(NT):
        p1_b(t)
        if t + 1 < NT:
            p1_a(t + 1)
        if t % 4 == 3:
            p2_part(t // 4)

    for m in range(8):
        def qsq(o, m=m):
            for nbq in range(4):
                r2 = 2 * (m * 2048 + nbq * 512)
                r = o.tensor_mul(PS1B[:, nbq * 512:(nbq + 1) * 512],
                                 BB[:, r2:r2 + 512], BB[:, r2:r2 + 512])
            return r
        s.op("vector", qsq, reads=list(bBs), writes=bPS1)
        for nb in range(4):
            s.op("tensor", lambda o, m=m, nb=nb: o.matmul(
                ps[4 + nb][0:16, :], SEGS[:, m * 16:(m + 1) * 16],
                PS1B[:, nb * 512:(nb + 1) * 512],
                start=(m == 0), stop=(m == 7)),
                reads=bPS1 + [bCONST], writes=[bP[4 + nb]])
    for nb in range(4):
        s.op("vector", lambda o, nb=nb: o.tensor_copy(
            QN[0:16, nb * 512:(nb + 1) * 512], ps[4 + nb][0:16, :]),
            reads=[bP[4 + nb]], writes=[bQN])
    s.op("vector", lambda o: o.reduce_max(QN[0:16, 0:1], QN[0:16, 0:2048],
                                          axis=AX.X),
         reads=[bQN], writes=[bQN])
    s.op("tensor", lambda o: o.transpose(ps[7][0:1, 0:16], QN[0:16, 0:1],
                                         IDT[0:16, 0:16]),
         reads=[bQN, bCONST], writes=[bP[7]])
    s.op("vector", lambda o: o.tensor_copy(ROWS[0:1, 0:16], ps[7][0:1, 0:16]),
         reads=[bP[7]], writes=[bROWS])

    for p in range(8):
        hA, hB = 2 * p, 2 * p + 1
        high = p >= 4
        def wu_fetch(pp):
            us = (pp % 2) * 512
            for wi, dsrc in enumerate((d_wku, d_wvu)):
                s.op("sync", dma(WU[:, us + wi * 256:us + wi * 256 + 256],
                                 dsrc[:, pp * 256:(pp + 1) * 256]),
                     writes=[bWU[pp % 2]])
        if p == 0:
            wu_fetch(0)
            wu_fetch(1)
        elif p < 7:
            wu_fetch(p + 1)
        uso = (p % 2) * 512
        for wi, (dst, dstbuf) in enumerate(((KP, [bKP]), (PS1, bPS1))):
            for nb in range(4):
                bank = nb

                def mmkv(o, wi=wi, nb=nb, bank=bank, uso=uso):
                    for kc in range(2):
                        r = o.matmul(
                            ps[bank][:],
                            WU[:, uso + wi * 256 + kc * 128:
                               uso + wi * 256 + kc * 128 + 128],
                            C[:, kc * 2048 + nb * 512:kc * 2048 + nb * 512 + 512],
                            start=(kc == 0), stop=(kc == 1))
                    return r
                s.op("tensor", mmkv, reads=[bWU[p % 2], bC],
                     writes=[bP[bank]])
                if dst is KP:
                    s.op("scalar", lambda o, nb=nb, bank=bank: o.copy(
                        KB[:, nb * 1024:nb * 1024 + 512], ps[bank][:]),
                        reads=[bP[bank]], writes=dstbuf)
                    s.op("vector", lambda o, nb=nb, bank=bank: o.tensor_sub(
                        KB[:, nb * 1024 + 512:nb * 1024 + 1024], ps[bank][:],
                        KB[:, nb * 1024:nb * 1024 + 512]),
                        reads=[bP[bank]] + dstbuf, writes=dstbuf)
                else:
                    s.op("vector", lambda o, dst=dst, nb=nb, bank=bank:
                         o.tensor_copy(dst[:, nb * 512:(nb + 1) * 512],
                                       ps[bank][:]),
                         reads=[bP[bank]], writes=dstbuf)
        def ksq(o):
            for nbq in range(4):
                r = o.tensor_mul(EBB[:, nbq * 512:(nbq + 1) * 512],
                                 KB[:, nbq * 1024:nbq * 1024 + 512],
                                 KB[:, nbq * 1024:nbq * 1024 + 512])
            return r
        s.op("vector", ksq, reads=[bKP], writes=list(bEB))
        for nb in range(4):
            s.op("tensor", lambda o, nb=nb: o.matmul(
                ps[4][0:2, :], SEGB[:, 0:2], EBB[:, nb * 512:(nb + 1) * 512],
                start=True, stop=True),
                reads=[bEB[0], bEB[1], bEB[2], bEB[3], bCONST],
                writes=[bP[4]])
            s.op("vector", lambda o, nb=nb: o.tensor_copy(
                KNS[0:2, nb * 512:(nb + 1) * 512], ps[4][0:2, :]),
                reads=[bP[4]], writes=[bKNS])
        s.op("vector", lambda o: o.reduce_max(KNS[0:2, 0:1], KNS[0:2, 0:2048],
                                              axis=AX.X),
             reads=[bKNS], writes=[bKNS])
        s.op("tensor", lambda o: o.transpose(ps[4][0:1, 0:2], KNS[0:2, 0:1],
                                             IDT[0:2, 0:2]),
             reads=[bKNS, bCONST], writes=[bP[4]])
        s.op("vector", lambda o: o.tensor_copy(ROWS[0:1, 16:18],
                                               ps[4][0:1, 0:2]),
             reads=[bP[4]], writes=[bROWS])
        s.op("vector", lambda o, p=p: o.tensor_mul(
            ROWS[0:1, 18:20], ROWS[0:1, 2 * p:2 * p + 2], ROWS[0:1, 16:18]),
            reads=[bROWS], writes=[bROWS])
        s.op("scalar", lambda o: o.activation(ROWS[0:1, 20:22],
                                              ROWS[0:1, 18:20], AF.Ln),
             reads=[bROWS], writes=[bROWS])
        s.op("scalar", lambda o: o.activation(ROWS[0:1, 22:24],
                                              ROWS[0:1, 20:22], AF.Exp,
                                              scale=0.5),
             reads=[bROWS], writes=[bROWS])
        s.op("vector", lambda o: o.tensor_scalar_mul(ROWS[0:1, 24:26],
                                                     ROWS[0:1, 22:24], -1.0),
             reads=[bROWS], writes=[bROWS])
        s.op("tensor", lambda o: o.matmul(ps[5][:, 0:2], ONE[0:1, :],
                                          ROWS[0:1, 24:26],
                                          start=True, stop=True),
             reads=[bROWS, bCONST], writes=[bP[5]])
        s.op("vector", lambda o, p=p: o.tensor_copy(
            NEGB[:, 2 * p:2 * p + 2], ps[5][:, 0:2]),
            reads=[bP[5]], writes=[bNEGB])
        for g in range(4):
            bank = 4 + (g % 2)

            def trv(o, g=g, bank=bank):
                for j in range(4):
                    r = o.transpose(
                        ps[bank][:, j * 128:(j + 1) * 128],
                        PS1[:, (g * 4 + j) * 128:(g * 4 + j + 1) * 128],
                        IDT[:])
                return r
            s.op("tensor", trv, reads=bPS1 + [bCONST], writes=[bP[bank]])
            ps3 = ps[bank].ap().rearrange("p (s c) -> p s c", c=128)

            def evv(o, g=g, bank=bank, ps3=ps3):
                o.tensor_copy(vb3[:, g * 4:(g + 1) * 4, 0:64],
                              ps3[:, :, 0:64])
                return o.tensor_copy(vb3[:, g * 4:(g + 1) * 4, 65:129],
                                     ps3[:, :, 64:128])
            s.op("vector", evv, reads=[bP[bank]], writes=[bVB])

        pend = []
        for qb in range(4):
            nkc = 4 * qb + 4
            pvbase = 4 + 2 * (qb % 2)

            def emit_s(kc, qb=qb, p=p):
                diag_i = kc - 4 * qb
                q0 = max(diag_i, 0) * 128
                sA_, sB_ = bP[kc % 2], bP[2 + kc % 2]
                pA_, pB_ = ps[kc % 2], ps[2 + kc % 2]

                def mms(o, kc=kc, qb=qb, p=p, q0=q0, pA_=pA_, pB_=pB_):
                    kb2 = (kc // 4) * 1024 + (kc % 4) * 128
                    qr2 = 2 * (p * 2048 + qb * 512)
                    for sl, pb_, rlo, rhi in ((0, pA_, 0, 64),
                                              (1, pB_, 64, 128)):
                        tp = (rlo, 0)
                        kh_ = KB[rlo:rhi, kb2:kb2 + 128]
                        kl_ = KB[rlo:rhi, kb2 + 512:kb2 + 640]
                        qh_ = BB[rlo:rhi, qr2 + q0:qr2 + 512]
                        ql_ = BB[rlo:rhi, qr2 + 512 + q0:qr2 + 1024]
                        o.matmul(pb_[:, q0:512], kh_, qh_, start=True,
                                 stop=False, tile_position=tp)
                        o.matmul(pb_[:, q0:512], kh_, ql_, start=False,
                                 stop=False, tile_position=tp)
                        r = o.matmul(pb_[:, q0:512], kl_, qh_, start=False,
                                     stop=True, tile_position=tp)
                    return r
                s.op("tensor", mms,
                     reads=[bKP] + bBs[4 * qb:4 * qb + 4], writes=[sA_, sB_])

            def emit_epv(kc, qb=qb, p=p, hA=hA, hB=hB, high=high,
                         pvbase=pvbase, nkc=nkc):
                diag_i = kc - 4 * qb
                q0 = max(diag_i, 0) * 128
                sA_, sB_ = bP[kc % 2], bP[2 + kc % 2]
                pA_, pB_ = ps[kc % 2], ps[2 + kc % 2]
                for sl, hh, sbank, pbank in ((0, hA, sA_, pA_),
                                             (1, hB, sB_, pB_)):
                    ebuf = bEB[sl * 2 + kc % 2]
                    eoff = sl * 1024 + (kc % 2) * 512
                    if diag_i < 0:
                        s.op("scalar", lambda o, eoff=eoff, pbank=pbank,
                             hh=hh: o.activation(
                                 EB[:, eoff:eoff + 512], pbank[:], AF.Exp,
                                 bias=NEGB[:, hh:hh + 1]),
                             reads=[sbank, bNEGB], writes=[ebuf])
                    else:
                        i = diag_i
                        if i < 3:
                            s.op("scalar", lambda o, eoff=eoff, pbank=pbank,
                                 hh=hh, i=i: o.activation(
                                     EB[:, eoff + (i + 1) * 128:eoff + 512],
                                     pbank[:, (i + 1) * 128:512], AF.Exp,
                                     bias=NEGB[:, hh:hh + 1]),
                                 reads=[sbank, bNEGB], writes=[ebuf])
                        s.op("scalar", lambda o, pbank=pbank, hh=hh, i=i,
                             sl=sl: o.activation(
                                 XT[:, 768 + sl * 128:768 + sl * 128 + 128],
                                 pbank[:, i * 128:(i + 1) * 128], AF.Exp,
                                 bias=NEGB[:, hh:hh + 1]),
                             reads=[sbank, bNEGB], writes=[bXT[0]])
                        s.op("vector", lambda o, eoff=eoff, i=i, sl=sl:
                             o.tensor_mul(
                                 EB[:, eoff + i * 128:eoff + (i + 1) * 128],
                                 XT[:, 768 + sl * 128:768 + sl * 128 + 128],
                                 TRI[:]),
                             reads=[bXT[0], bCONST], writes=[ebuf])
                for sl, pvbank in ((0, pvbase), (1, pvbase + 1)):
                    ebuf = bEB[sl * 2 + kc % 2]
                    eoff = sl * 1024 + (kc % 2) * 512

                    def mmpv(o, kc=kc, sl=sl, pvbank=pvbank, eoff=eoff,
                             q0=q0, nkc=nkc):
                        return o.matmul(
                            ps[pvbank][0:65, q0:512],
                            VB[:, kc * 130 + sl * 65:
                               kc * 130 + sl * 65 + 65],
                            EB[:, eoff + q0:eoff + 512],
                            start=(kc == 0), stop=(kc == nkc - 1))
                    s.op("tensor", mmpv, reads=[bVB, ebuf],
                         writes=[bP[pvbank]])

            for kc in range(nkc):
                emit_s(kc)
                if kc >= 1:
                    emit_epv(kc - 1)
                if kc == 1 and pend:
                    for fn in pend:
                        fn()
                    pend = []
            emit_epv(nkc - 1)
            def tail(qb=qb, pvbase=pvbase):
                s.op("vector", lambda o: o.tensor_copy(
                    DRS[64:65, 0:512], ps[pvbase][64:65, :]),
                    reads=[bP[pvbase]], writes=[bDRS])
                s.op("vector", lambda o: o.tensor_copy(
                    DRS[64:65, 512:1024], ps[pvbase + 1][64:65, :]),
                    reads=[bP[pvbase + 1]], writes=[bDRS])
                s.op("scalar", lambda o: o.activation(
                    DRS[64:65, 1024:2048], DRS[64:65, 0:1024],
                    AF.Ln), reads=[bDRS], writes=[bDRS])
                s.op("scalar", lambda o: o.activation(
                    DRS[64:65, 0:1024], DRS[64:65, 1024:2048],
                    AF.Exp, scale=-1.0), reads=[bDRS], writes=[bDRS])
                for sl in range(2):
                    src = DRS[64:65, sl * 512:sl * 512 + 512].rearrange(
                        "p (u n) -> p u n", u=1).broadcast_to([1, 64, 512])
                    s.op("gpsimd", lambda o, sl=sl, src=src: o.dma_start(
                        out=PS1[0:64, 1024 + sl * 512:1024 + sl * 512 + 512],
                        in_=src), reads=[bDRS], writes=[bPS1h[1]])
                for sl, hh in ((0, hA), (1, hB)):
                    hlow = hh % 8
                    if not high:
                        s.op("vector", lambda o, sl=sl, hlow=hlow,
                             pvbank=pvbase + sl, qb=qb: o.tensor_mul(
                                 A[0:64, hlow * 2048 + qb * 512:
                                   hlow * 2048 + qb * 512 + 512],
                                 ps[pvbank][0:64, :],
                                 PS1[0:64, 1024 + sl * 512:
                                     1024 + sl * 512 + 512]),
                             reads=[bP[pvbase + sl], bPS1h[1]],
                             writes=list(bAs))
                    else:
                        s.op("vector", lambda o, sl=sl,
                             pvbank=pvbase + sl: o.tensor_mul(
                                 PS1[0:64, sl * 512:sl * 512 + 512],
                                 ps[pvbank][0:64, :],
                                 PS1[0:64, 1024 + sl * 512:
                                     1024 + sl * 512 + 512]),
                             reads=[bP[pvbase + sl], bPS1h[1]],
                             writes=[bPS1h[0]])
                        s.op("gpsimd", lambda o, sl=sl, hlow=hlow, qb=qb:
                             o.dma_start(
                                 out=A[64:128, hlow * 2048 + qb * 512:
                                       hlow * 2048 + qb * 512 + 512],
                                 in_=PS1[0:64, sl * 512:sl * 512 + 512]),
                             reads=[bPS1h[0]], writes=list(bAs))
            pend.append(tail)
        for fn in pend:
            fn()
        pend = []

    oT_of = lambda kc, nb: A[:, kc * 2048 + nb * 512:kc * 2048 + nb * 512 + 512]
    for m in range(8):
        slot = m % 2
        s.op("sync", dma(WS[:, slot * 1024:slot * 1024 + 1024],
                         d_wout[:, m * 1024:(m + 1) * 1024]),
             writes=[bWS[slot]])
        proj_mblock(lambda nb, m=m: B[:, m * 2048 + nb * 512:
                                      m * 2048 + nb * 512 + 512],
                    lambda nb: bBs[4 * nb:4 * nb + 4], slot,
                    lambda nb: list(bAs), oT_of, 8)

    def ln_stats_ps(bufs, b0, b1, sl):
        mo, no = sl * 4, sl * 12
        s.op("vector", lambda o: (
            o.bn_stats(NS[:, no:no + 6], b0),
            o.bn_stats(NS[:, no + 6:no + 12], b1))[-1],
            reads=bufs, writes=[bNS[sl]])
        s.op("vector", lambda o: o.bn_aggr(MV[:, mo:mo + 2],
                                           NS[:, no:no + 12]),
             reads=[bNS[sl]], writes=[bMV[sl]])
        s.op("scalar", lambda o: o.activation(MV[:, mo + 3:mo + 4],
                                              MV[:, mo + 1:mo + 2], AF.Ln,
                                              bias=EPSC[:, 0:1], scale=1.0),
             reads=[bMV[sl], bCONST], writes=[bMV[sl]])
        s.op("scalar", lambda o: o.activation(MV[:, mo + 2:mo + 3],
                                              MV[:, mo + 3:mo + 4], AF.Exp,
                                              scale=-0.5),
             reads=[bMV[sl]], writes=[bMV[sl]])

    def p5_a(t):
        par = t % 2
        s.op("gpsimd", dma(XT[:, par * 1024:par * 1024 + 1024],
                           d_x[t * 128:(t + 1) * 128, :]), writes=[bXT[par]])
        for hf in range(2):
            bank = par * 2 + hf

            def tra(o, hf=hf, t=t, bank=bank):
                for j in range(4):
                    r = o.transpose(ps[bank][:, j * 128:(j + 1) * 128],
                                    B[:, (hf * 4 + j) * 2048 + t * 128:
                                      (hf * 4 + j) * 2048 + t * 128 + 128],
                                    IDT[:])
                return r
            s.op("tensor", tra, reads=[bBs[t], bCONST], writes=[bP[bank]])
        ln_stats_ps([bP[par * 2], bP[par * 2 + 1]],
                    ps[par * 2][:], ps[par * 2 + 1][:], par)

    def p5_b(t):
        par = t % 2
        mo = par * 4
        eoff = par * 1024
        for hf in range(2):
            bank = par * 2 + hf
            s.op("vector", lambda o, mo=mo, eoff=eoff, hf=hf, bank=bank:
                 o.tensor_scalar(
                     EB[:, eoff + hf * 512:eoff + hf * 512 + 512],
                     ps[bank][:], MV[:, mo:mo + 1], MV[:, mo + 2:mo + 3],
                     ALU.subtract, ALU.mult),
                 reads=[bP[bank], bMV[par]],
                 writes=[bEB[2 * par + hf]])
        s.op("vector", lambda o, t=t, par=par, eoff=eoff: o.tensor_add(
            A[:, t * 1024:(t + 1) * 1024], EB[:, eoff:eoff + 1024],
            XT[:, par * 1024:par * 1024 + 1024]),
            reads=[bEB[2 * par], bEB[2 * par + 1], bXT[par]],
            writes=[bAs[t]])
        ln_stats(bAs[t], A[:, t * 1024:(t + 1) * 1024], 2 + par)

    def p5_c(t):
        par = t % 2
        mo = (2 + par) * 4
        s.op("vector", lambda o, t=t, mo=mo, par=par: o.tensor_scalar(
            KP[:, par * 1024:par * 1024 + 1024],
            A[:, t * 1024:(t + 1) * 1024], MV[:, mo:mo + 1],
            MV[:, mo + 2:mo + 3], ALU.subtract, ALU.mult),
            reads=[bAs[t], bMV[2 + par]], writes=[bKP])
        tr_tile_to(lambda f, tt: B[:, f * 2048 + tt * 128:
                                   f * 2048 + tt * 128 + 128],
                   t, bKP, KP, (4 + par * 2, 5 + par * 2), [bBs[t]],
                   srcoff=par * 1024, dstbuf_is_list=True)

    def gate_mm(nb):
        def mmgate(o, nb=nb):
            for kc in range(8):
                r = o.matmul(ps[7][0:NE, :], WS[:, kc * 8:(kc + 1) * 8],
                             ln3_of(kc, nb), start=(kc == 0), stop=(kc == 7))
            return r
        s.op("tensor", mmgate, reads=[bWS[0]] + bBs[4 * nb:4 * nb + 4],
             writes=[bP[7]])
        s.op("vector", lambda o, nb=nb: o.tensor_scalar_add(
            LG[0:NE, nb * 512:(nb + 1) * 512], ps[7][0:NE, :], BG[0:NE, 0:1]),
            reads=[bP[7], bCONST], writes=[bLG])

    s.op("sync", dma(WS[:, 0:64], d_wg[:, :]), writes=[bWS[0]])
    ln3_of = lambda kc, nb: B[:, kc * 2048 + nb * 512:
                              kc * 2048 + nb * 512 + 512]
    p5_a(0)
    p5_a(1)
    p5_b(0)
    for t in range(NT):
        if t + 1 < NT:
            p5_b(t + 1)
        p5_c(t)
        if t + 2 < NT:
            p5_a(t + 2)
        if t % 4 == 3:
            gate_mm(t // 4)

    for t in range(NT):
        s.op("tensor", lambda o, t=t: o.transpose(
            ps[4][:, 0:NE], LG[0:NE, t * 128:(t + 1) * 128],
            IDT[0:NE, 0:NE]), reads=[bLG, bCONST], writes=[bP[4]])
        s.op("vector", lambda o, t=t: o.tensor_copy(
            LGT[:, t * NE:(t + 1) * NE], ps[4][:, 0:NE]),
            reads=[bP[4]], writes=[bLGT])
    for t in range(NT):
        s.op("vector", lambda o, t=t: o.reduce_max(
            GSS[:, 16 + t:17 + t], LGT[:, t * NE:(t + 1) * NE], axis=AX.X),
            reads=[bLGT], writes=[bGSS])
        s.op("vector", lambda o, t=t: o.tensor_scalar_sub(
            GES[:, t * NE:(t + 1) * NE], LGT[:, t * NE:(t + 1) * NE],
            GSS[:, 16 + t:17 + t]), reads=[bLGT, bGSS], writes=[bGES])
    s.op("scalar", lambda o: o.activation(LGT[:], GES[:], AF.Exp),
         reads=[bGES], writes=[bLGT])
    for t in range(NT):
        s.op("vector", lambda o, t=t: o.reduce_sum(
            GSS[:, t:t + 1], LGT[:, t * NE:(t + 1) * NE], axis=AX.X),
            reads=[bLGT], writes=[bGSS])
    s.op("scalar", lambda o: o.activation(GSS[:, 16:32], GSS[:, 0:16], AF.Ln),
         reads=[bGSS], writes=[bGSS])
    s.op("scalar", lambda o: o.activation(GSS[:, 0:16], GSS[:, 16:32],
                                          AF.Exp, scale=-1.0),
         reads=[bGSS], writes=[bGSS])
    for t in range(NT):
        s.op("tensor", lambda o, t=t: o.transpose(
            ps[4][0:1, 0:128], GSS[:, 0 + t:1 + t], IDT[:]),
            reads=[bGSS, bCONST], writes=[bP[4]])
        s.op("vector", lambda o, t=t: o.tensor_copy(
            BPROW[0:1, t * 128:(t + 1) * 128], ps[4][0:1, 0:128]),
            reads=[bP[4]], writes=[bBPROW])
    for nb in range(4):
        s.op("tensor", lambda o, nb=nb: o.matmul(
            ps[5][:], ONE[0:1, :], BPROW[0:1, nb * 512:(nb + 1) * 512],
            start=True, stop=True), reads=[bBPROW, bCONST], writes=[bP[5]])
        s.op("vector", lambda o, nb=nb: o.tensor_copy(
            KP[:, nb * 512:(nb + 1) * 512], ps[5][:]),
            reads=[bP[5]], writes=[bKP])
    for s4 in range(4):
        for mf in range(8):
            slot = mf % 2
            s.op("sync", dma(WS[:, slot * 1024:slot * 1024 + 1024],
                             d_we[:, mf * 1024:(mf + 1) * 1024]),
                 writes=[bWS[slot]])
            bank = 4 + mf % 4

            def mme(o, mf=mf, s4=s4, slot=slot, bank=bank):
                for kc in range(8):
                    r = o.matmul(ps[bank][:],
                                 WS[:, slot * 1024 + kc * 128:
                                    slot * 1024 + kc * 128 + 128],
                                 ln3_of(kc, s4), start=(kc == 0),
                                 stop=(kc == 7))
                return r
            s.op("tensor", mme, reads=[bWS[slot]] + bBs[4 * s4:4 * s4 + 4],
                 writes=[bP[bank]])
            s.op("vector", lambda o, mf=mf, s4=s4, bank=bank:
                 o.scalar_tensor_tensor(
                     C[:, mf * 512:(mf + 1) * 512], ps[bank][:],
                     BE[:, mf:mf + 1], KP[:, s4 * 512:(s4 + 1) * 512],
                     ALU.add, ALU.mult),
                 reads=[bP[bank], bCONST, bKP], writes=[bC])
        def p6_a(tt, s4=s4):
            t = s4 * 4 + tt
            par = t % 2
            for hf in range(2):
                bank = par * 2 + hf

                def trf(o, hf=hf, tt=tt, bank=bank):
                    for j in range(4):
                        f = hf * 4 + j
                        r = o.transpose(ps[bank][:, j * 128:(j + 1) * 128],
                                        C[:, f * 512 + tt * 128:
                                          f * 512 + tt * 128 + 128], IDT[:])
                    return r
                s.op("tensor", trf, reads=[bC, bCONST], writes=[bP[bank]])
            ln_stats_ps([bP[par * 2], bP[par * 2 + 1]],
                        ps[par * 2][:], ps[par * 2 + 1][:], par)

        def p6_b(tt, s4=s4):
            t = s4 * 4 + tt
            par = t % 2
            mo = par * 4
            for hf in range(2):
                bank = par * 2 + hf
                s.op("vector", lambda o, mo=mo, par=par, hf=hf, bank=bank:
                     o.tensor_scalar(
                         EB[:, par * 1024 + hf * 512:
                            par * 1024 + hf * 512 + 512],
                         ps[bank][:], MV[:, mo:mo + 1], MV[:, mo + 2:mo + 3],
                         ALU.subtract, ALU.mult),
                     reads=[bP[bank], bMV[par]],
                     writes=[bEB[2 * par + hf]])
            s.op("vector", lambda o, t=t, par=par: o.tensor_add(
                XT[:, par * 1024:par * 1024 + 1024],
                EB[:, par * 1024:par * 1024 + 1024],
                A[:, t * 1024:(t + 1) * 1024]),
                reads=[bEB[2 * par], bEB[2 * par + 1], bAs[t]],
                writes=[bXT[par]])
            s.op("gpsimd", lambda o, t=t, par=par: o.dma_start(
                out=d_y[t * 128:(t + 1) * 128, :],
                in_=XT[:, par * 1024:par * 1024 + 1024]),
                reads=[bXT[par]], writes=[bY])

        p6_a(0)
        p6_a(1)
        for tt in range(4):
            p6_b(tt)
            if tt + 2 < 4:
                p6_a(tt + 2)

    for i in range(NDMASEM):
        key = f"gpsimd{i}"
        fv = s.count[key]
        if fv:
            s.raw("gpsimd", lambda o, key=key, fv=fv: o.wait_ge(
                s.sems[key], fv))

    block = es.enter_context(nc.Block())
    with es:
        for ename in ENGINES:
            getattr(block, ename)(
                (lambda prog: lambda obj: [f(obj) for f in prog])(
                    s.prog[ename]))
    return nc


_NC_CACHE = []


def prepare_inputs(L0_attn_x_bld, L0_attn_Wq, L0_attn_Wkv_down,
                   L0_attn_Wk_up, L0_attn_Wv_up, L0_attn_Wout, L0_ffn_Wg,
                   L0_ffn_bg, L0_ffn_We, L0_ffn_be):
    def streamed(w, nm):
        k = w.shape[0]
        return np.ascontiguousarray(
            w.reshape(k // 128, 128, nm, 128).transpose(1, 2, 0, 3)
            .reshape(128, nm * (k // 128) * 128))

    x = np.ascontiguousarray(np.asarray(L0_attn_x_bld, dtype=np.float32))
    wq = streamed(np.asarray(L0_attn_Wq, dtype=np.float32).reshape(DM, DM), 8)
    wkv = streamed(np.asarray(L0_attn_Wkv_down, dtype=np.float32), 2)
    wku = streamed(np.asarray(L0_attn_Wk_up,
                              dtype=np.float32).reshape(LD, DM), 8)
    wvu = streamed(np.asarray(L0_attn_Wv_up,
                              dtype=np.float32).reshape(LD, DM), 8)
    wout0 = np.asarray(L0_attn_Wout, dtype=np.float32).reshape(DM, DM)
    wout = streamed(np.ascontiguousarray(np.concatenate(
        [np.concatenate([wout0[h * 64:(h + 1) * 64],
                         wout0[(h + 8) * 64:(h + 9) * 64]], axis=0)
         for h in range(8)], axis=0)), 8)
    wg = np.ascontiguousarray(
        np.asarray(L0_ffn_Wg, dtype=np.float32).reshape(8, 128, NE)
        .transpose(1, 0, 2).reshape(128, 64))
    bgc = np.ascontiguousarray(np.asarray(L0_ffn_bg,
                                          dtype=np.float32).reshape(NE, 1))
    we = streamed(np.asarray(L0_ffn_We, dtype=np.float32), 8)
    bet = np.ascontiguousarray(np.asarray(L0_ffn_be,
                                          dtype=np.float32).reshape(NFC,
                                                                    128).T)
    cid = np.eye(128, dtype=np.float32)
    cones = np.ones((128, 128), dtype=np.float32)
    ctri = np.triu(np.ones((128, 128), dtype=np.float32))
    cseg = np.zeros((128, 2), dtype=np.float32)
    cseg[0:64, 0] = 1.0
    cseg[64:128, 1] = 1.0
    import ml_dtypes
    csegs = np.zeros((128, 128), dtype=ml_dtypes.bfloat16)
    for m in range(8):
        csegs[0:64, m * 16 + 2 * m] = 1.0
        csegs[64:128, m * 16 + 2 * m + 1] = 1.0
    csegb = np.ascontiguousarray(cseg.astype(ml_dtypes.bfloat16))

    shared = {"wq": wq, "wkv": wkv, "wku": wku, "wvu": wvu, "wout": wout,
              "wg": wg, "bgc": bgc, "we": we, "bet": bet, "cid": cid,
              "cones": cones, "ctri": ctri, "cseg": cseg, "csegs": csegs, "csegb": csegb}
    return shared, x


def kernel(**inputs):
    shared, x = prepare_inputs(**inputs)
    if not _NC_CACHE:
        _NC_CACHE.append(build_nc())
    nc = _NC_CACHE[0]
    in_maps = [dict(shared, x=np.ascontiguousarray(x[b])) for b in range(8)]
    res = run_bass_kernel_spmd(nc, in_maps, core_ids=list(range(8)))
    out = np.stack([np.asarray(r["y"], dtype=np.float32)
                    for r in res.results], axis=0)
    return out


# version 19
# speedup vs baseline: 1.0001x; 1.0001x over previous
import numpy as np
from contextlib import ExitStack

import concourse.bass as bass
import concourse.mybir as mybir
from concourse.bass_utils import run_bass_kernel_spmd

F32 = mybir.dt.float32
BF16 = mybir.dt.bfloat16
AF = mybir.ActivationFunctionType
ALU = mybir.AluOpType
AX = mybir.AxisListType

L, DM, H, HD, LD, NE = 2048, 1024, 16, 64, 256, 8
NT = L // 128
NFC = DM // 128
LNEPS = 1e-5

ENGINES = ("tensor", "vector", "scalar", "sync", "gpsimd")
DMA_ENGINES = ("sync", "gpsimd")
NDMASEM = 4
SEMKEYS = ["tensor", "vector", "scalar"] + [
    f"{e}{i}" for e in DMA_ENGINES for i in range(NDMASEM)]


class Buf:
    __slots__ = ("name", "w", "r")

    def __init__(self, name):
        self.name = name
        self.w = {}
        self.r = {}


class Sched:
    def __init__(self):
        self.prog = {e: [] for e in ENGINES}
        self.count = {k: 0 for k in SEMKEYS}
        self.bound = {e: {k: 0 for k in SEMKEYS} for e in ENGINES}
        self.dma_rr = {e: 0 for e in DMA_ENGINES}
        self.sems = {}

    def op(self, eng, build, reads=(), writes=()):
        deps = {}
        for b in reads:
            for we, wv in b.w.items():
                deps[we] = max(deps.get(we, 0), wv)
        for b in writes:
            for we, wv in b.w.items():
                deps[we] = max(deps.get(we, 0), wv)
            for re, rv in b.r.items():
                deps[re] = max(deps.get(re, 0), rv)
        if eng in DMA_ENGINES:
            key = f"{eng}{self.dma_rr[eng] % NDMASEM}"
            self.dma_rr[eng] += 1
            step = 16
            deps[key] = max(deps.get(key, 0), self.count[key])
        else:
            key, step = eng, 1
        waits = []
        for de, dv in deps.items():
            if self.bound[eng][de] >= dv:
                continue
            waits.append((de, dv))
            self.bound[eng][de] = dv
        self.count[key] += step
        v = self.count[key]
        sems = self.sems

        def thunk(obj, _waits=tuple(waits), _build=build, _key=key,
                  _step=step):
            for de, dv in _waits:
                obj.wait_ge(sems[de], dv)
            last = _build(obj)
            last.then_inc(sems[_key], _step)

        self.prog[eng].append(thunk)
        for b in writes:
            b.w = {key: v}
            b.r = {}
        for b in reads:
            b.r[key] = max(b.r.get(key, 0), v)
        return v

    def raw(self, eng, build):
        self.prog[eng].append(lambda obj, _b=build: _b(obj))


def build_nc():
    nc = bass.Bass("TRN2", target_bir_lowering=False)
    ap = lambda t: t.ap()
    d_x = ap(nc.dram_tensor("x", [L, DM], F32, kind="ExternalInput"))
    d_wq = ap(nc.dram_tensor("wq", [128, 8192], F32, kind="ExternalInput"))
    d_wkv = ap(nc.dram_tensor("wkv", [128, 2048], F32, kind="ExternalInput"))
    d_wku = ap(nc.dram_tensor("wku", [128, 2048], F32, kind="ExternalInput"))
    d_wvu = ap(nc.dram_tensor("wvu", [128, 2048], F32, kind="ExternalInput"))
    d_wout = ap(nc.dram_tensor("wout", [128, 8192], F32, kind="ExternalInput"))
    d_wg = ap(nc.dram_tensor("wg", [128, 64], F32, kind="ExternalInput"))
    d_bg = ap(nc.dram_tensor("bgc", [NE, 1], F32, kind="ExternalInput"))
    d_we = ap(nc.dram_tensor("we", [128, 8192], F32, kind="ExternalInput"))
    d_bet = ap(nc.dram_tensor("bet", [128, NFC], F32, kind="ExternalInput"))
    d_id = ap(nc.dram_tensor("cid", [128, 128], F32, kind="ExternalInput"))
    d_one = ap(nc.dram_tensor("cones", [128, 128], F32, kind="ExternalInput"))
    d_tri = ap(nc.dram_tensor("ctri", [128, 128], F32, kind="ExternalInput"))
    d_seg = ap(nc.dram_tensor("cseg", [128, 2], F32, kind="ExternalInput"))
    d_segs = ap(nc.dram_tensor("csegs", [128, 128], BF16,
                               kind="ExternalInput"))
    d_segb = ap(nc.dram_tensor("csegb", [128, 2], BF16,
                               kind="ExternalInput"))
    d_y = ap(nc.dram_tensor("y", [L, DM], F32, kind="ExternalOutput"))

    es = ExitStack()
    sb = lambda name, fr: es.enter_context(
        nc.sbuf_tensor(name, [128, fr], F32))
    A = sb("A", 16384)
    B = sb("B", 16384)
    C = sb("C", 4096)
    KP = sb("KP", 2048)
    PS1 = sb("PS1", 2048)
    VB = sb("VB", 2080)
    EB = sb("EB", 2048)
    WS = sb("WS", 2048)
    WU = sb("WU", 1024)
    XT = sb("XT", 2048)
    SCR = sb("SCR", 2048)
    IDT = sb("IDT", 128)
    ONE = sb("ONE", 128)
    TRI = sb("TRI", 128)
    SEG = sb("SEG", 2)
    SEGS = es.enter_context(nc.sbuf_tensor("SEGS", [128, 128], BF16))
    SEGB = es.enter_context(nc.sbuf_tensor("SEGB", [128, 2], BF16))
    BE = sb("BE", NFC)
    BG = sb("BG", 1)
    ROWS = sb("ROWS", 32)
    NEGB = sb("NEGB", 16)
    MV = sb("MV", 16)
    EPSC = sb("EPSC", 1)
    NS = sb("NS", 48)
    LGT = sb("LGT", 128)
    GES = sb("GES", 128)
    GSS = sb("GSS", 32)
    QN = KNS = DRS = LG = BPROW = SCR

    ps = [es.enter_context(nc.psum_tensor(f"p{i}", [128, 512], F32))
          for i in range(8)]

    PS1B = nc.alloc_sbuf_tensor_at("PS1B", [128, 4096], BF16,
                                   offset=nc.lookup_mloc(PS1).addr)
    KB = nc.alloc_sbuf_tensor_at("KB", [128, 4096], BF16,
                                 offset=nc.lookup_mloc(KP).addr)
    BB = nc.alloc_sbuf_tensor_at("BB", [128, 32768], BF16,
                                 offset=nc.lookup_mloc(B).addr)
    EBB = nc.alloc_sbuf_tensor_at("EBB", [128, 4096], BF16,
                                  offset=nc.lookup_mloc(EB).addr)
    s = Sched()
    for k in SEMKEYS:
        s.sems[k] = es.enter_context(nc.semaphore(f"sem_{k}"))

    bC, bKP, bVB = (Buf(n) for n in "C KP VB".split())
    bPS1h = [Buf("PS1a"), Buf("PS1b")]
    bPS1 = bPS1h
    bAs = [Buf(f"A{t}") for t in range(NT)]
    bBs = [Buf(f"B{t}") for t in range(NT)]
    bEB = [Buf(f"EB{i}") for i in range(4)]
    bWS = [Buf("WS0"), Buf("WS1")]
    bWU = [Buf("WU0"), Buf("WU1")]
    bXT = [Buf("XT0"), Buf("XT1")]
    bP = [Buf(f"p{i}") for i in range(8)]
    bCONST = Buf("CONST")
    bSCR = Buf("SCR")
    bQN = bKNS = bDRS = bLG = bBPROW = bSCR
    bROWS, bNEGB = Buf("ROWS"), Buf("NEGB")
    bMV = [Buf(f"MV{i}") for i in range(4)]
    bNS = [Buf(f"NS{i}") for i in range(4)]
    bLGT, bGES, bGSS, bDSCR = (Buf(n) for n in "LGT GES GSS DSCR".split())
    bY = Buf("Y")

    def dma(out, in_):
        return lambda o: o.dma_start(out=out, in_=in_)

    s.op("sync", dma(IDT[:], d_id[:, :]), writes=[bCONST])
    s.op("sync", dma(ONE[:], d_one[:, :]), writes=[bCONST])
    s.op("sync", dma(TRI[:], d_tri[:, :]), writes=[bCONST])
    s.op("sync", dma(SEG[:], d_seg[:, :]), writes=[bCONST])
    s.op("sync", dma(SEGS[:], d_segs[:, :]), writes=[bCONST])
    s.op("sync", dma(SEGB[:], d_segb[:, :]), writes=[bCONST])
    s.op("vector", lambda o: o.memset(EPSC[:], LNEPS), writes=[bCONST])
    s.op("sync", dma(BE[:], d_bet[:, :]), writes=[bCONST])
    s.op("vector", lambda o: o.memset(VB[:], 0.0), writes=[bVB])
    vb3 = VB.ap().rearrange("p (s c) -> p s c", c=130)
    s.op("vector", lambda o: o.memset(vb3[:, :, 64:65], 1.0), writes=[bVB])
    s.op("vector", lambda o: o.memset(vb3[:, :, 129:130], 1.0), writes=[bVB])
    s.op("sync", dma(BG[0:NE, :], d_bg[:, :]), writes=[bCONST])

    def ln_stats(xbuf, xslice, par=0):
        mo, no = par * 4, par * 12
        s.op("vector", lambda o: (
            o.bn_stats(NS[:, no:no + 6], xslice[:, 0:512]),
            o.bn_stats(NS[:, no + 6:no + 12], xslice[:, 512:1024]))[-1],
            reads=[xbuf], writes=[bNS[par]])
        s.op("vector", lambda o: o.bn_aggr(MV[:, mo:mo + 2],
                                           NS[:, no:no + 12]),
             reads=[bNS[par]], writes=[bMV[par]])
        s.op("scalar", lambda o: o.activation(MV[:, mo + 3:mo + 4],
                                              MV[:, mo + 1:mo + 2], AF.Ln,
                                              bias=EPSC[:, 0:1], scale=1.0),
             reads=[bMV[par], bCONST], writes=[bMV[par]])
        s.op("scalar", lambda o: o.activation(MV[:, mo + 2:mo + 3],
                                              MV[:, mo + 3:mo + 4], AF.Exp,
                                              scale=-0.5),
             reads=[bMV[par]], writes=[bMV[par]])

    def tr_tile_to(dstT, t, srcbuf, srctensor, banks, dstbuf, srcoff=0,
                   dstbuf_is_list=False):
        src = srctensor if srctensor is not None else PS1
        for hf in range(2):
            bank = banks[hf]

            def trf(o, hf=hf, bank=bank):
                for j in range(4):
                    r = o.transpose(
                        ps[bank][:, j * 128:(j + 1) * 128],
                        src[:, srcoff + (hf * 4 + j) * 128:
                            srcoff + (hf * 4 + j + 1) * 128],
                        IDT[:])
                return r
            s.op("tensor", trf, reads=[srcbuf, bCONST], writes=[bP[bank]])

            def evf(o, hf=hf, bank=bank, t=t):
                for j in range(4):
                    f = hf * 4 + j
                    r = o.copy(dstT(f, t), ps[bank][:, j * 128:(j + 1) * 128])
                return r
            s.op("scalar", evf, reads=[bP[bank]],
                 writes=dstbuf if dstbuf_is_list else [dstbuf])

    def p1_a(t):
        par = t % 2
        xs_ = XT[:, par * 1024:par * 1024 + 1024]
        s.op("gpsimd", dma(xs_, d_x[t * 128:(t + 1) * 128, :]),
             writes=[bXT[par]])
        ln_stats(bXT[par], xs_, par)

    def p1_b(t):
        par = t % 2
        xs_ = XT[:, par * 1024:par * 1024 + 1024]
        wk_ = PS1[:, par * 1024:par * 1024 + 1024]
        mo = par * 4
        s.op("vector", lambda o, xs_=xs_, wk_=wk_, mo=mo: o.tensor_scalar(
            wk_, xs_, MV[:, mo:mo + 1], MV[:, mo + 2:mo + 3],
            ALU.subtract, ALU.mult),
            reads=[bXT[par], bMV[par]], writes=[bPS1h[par]])
        tr_tile_to(lambda f, tt: A[:, f * 2048 + tt * 128:
                                   f * 2048 + tt * 128 + 128],
                   t, bPS1h[par], PS1, (par * 2, par * 2 + 1),
                   [bAs[t]], srcoff=par * 1024, dstbuf_is_list=True)

    def proj_group(dst, dstbufs, wslot, rhsbufs, rhs_of, kchunks, nb, bank,
                   scale=None, evac=None):
        def mmg(o, nb=nb, bank=bank):
            for kc in range(kchunks):
                r = o.matmul(
                    ps[bank][:],
                    WS[:, wslot * 1024 + kc * 128:
                       wslot * 1024 + kc * 128 + 128],
                    rhs_of(kc, nb),
                    start=(kc == 0), stop=(kc == kchunks - 1))
            return r
        s.op("tensor", mmg, reads=[bWS[wslot]] + rhsbufs(nb),
             writes=[bP[bank]])
        if evac is not None:
            evac(bank)
        elif scale is None:
            s.op("scalar", lambda o, nb=nb, bank=bank: o.copy(
                dst(nb), ps[bank][:]), reads=[bP[bank]],
                writes=dstbufs(nb))
        else:
            s.op("scalar", lambda o, nb=nb, bank=bank: o.activation(
                dst(nb), ps[bank][:], AF.Copy, scale=scale),
                reads=[bP[bank]], writes=dstbufs(nb))

    def proj_mblock(dst, dstbufs, wslot, rhsbufs, rhs_of, kchunks, scale=None,
                    banks=(0, 1, 2, 3)):
        for nb in range(4):
            proj_group(dst, dstbufs, wslot, rhsbufs, rhs_of, kchunks, nb,
                       banks[nb % len(banks)], scale)

    lnT_of = lambda kc, nb: A[:, kc * 2048 + nb * 512:
                              kc * 2048 + nb * 512 + 512]

    def p2_part(nb):
        for m in range(2):
            slot = m % 2
            s.op("sync", dma(WS[:, slot * 1024:slot * 1024 + 1024],
                             d_wkv[:, m * 1024:(m + 1) * 1024]),
                 writes=[bWS[slot]])
            proj_group(lambda nb, m=m: C[:, m * 2048 + nb * 512:
                                         m * 2048 + nb * 512 + 512],
                       lambda nb: [bC], slot,
                       lambda nb: bAs[4 * nb:4 * nb + 4], lnT_of, 8, nb,
                       4 + m % 4)
        for m in range(8):
            slot = m % 2
            s.op("sync", dma(WS[:, slot * 1024:slot * 1024 + 1024],
                             d_wq[:, m * 1024:(m + 1) * 1024]),
                 writes=[bWS[slot]])
            bank = 4 + (m + 2) % 4

            def evq(bank, m=m, nb=nb):
                r2 = 2 * (m * 2048 + nb * 512)
                qh = BB[:, r2:r2 + 512]
                ql = BB[:, r2 + 512:r2 + 1024]
                dd = bBs[4 * nb:4 * nb + 4]
                s.op("scalar", lambda o, bank=bank, qh=qh: o.activation(
                    qh, ps[bank][:], AF.Copy, scale=0.125),
                    reads=[bP[bank]], writes=dd)
                s.op("vector", lambda o, bank=bank, qh=qh, ql=ql:
                     o.scalar_tensor_tensor(ql, ps[bank][:], 0.125, qh,
                                            ALU.mult, ALU.subtract),
                     reads=[bP[bank]] + dd, writes=dd)
            proj_group(None, None, slot,
                       lambda nb: bAs[4 * nb:4 * nb + 4], lnT_of, 8, nb,
                       bank, evac=evq)

    p1_a(0)
    for t in range(NT):
        p1_b(t)
        if t + 1 < NT:
            p1_a(t + 1)
        if t % 4 == 3:
            p2_part(t // 4)

    for m in range(8):
        def qsq(o, m=m):
            for nbq in range(4):
                r2 = 2 * (m * 2048 + nbq * 512)
                r = o.tensor_mul(PS1B[:, nbq * 512:(nbq + 1) * 512],
                                 BB[:, r2:r2 + 512], BB[:, r2:r2 + 512])
            return r
        s.op("vector", qsq, reads=list(bBs), writes=bPS1)
        for nb in range(4):
            s.op("tensor", lambda o, m=m, nb=nb: o.matmul(
                ps[4 + nb][0:16, :], SEGS[:, m * 16:(m + 1) * 16],
                PS1B[:, nb * 512:(nb + 1) * 512],
                start=(m == 0), stop=(m == 7)),
                reads=bPS1 + [bCONST], writes=[bP[4 + nb]])
    for nb in range(4):
        s.op("vector", lambda o, nb=nb: o.tensor_copy(
            QN[0:16, nb * 512:(nb + 1) * 512], ps[4 + nb][0:16, :]),
            reads=[bP[4 + nb]], writes=[bQN])
    s.op("vector", lambda o: o.reduce_max(QN[0:16, 0:1], QN[0:16, 0:2048],
                                          axis=AX.X),
         reads=[bQN], writes=[bQN])
    s.op("tensor", lambda o: o.transpose(ps[7][0:1, 0:16], QN[0:16, 0:1],
                                         IDT[0:16, 0:16]),
         reads=[bQN, bCONST], writes=[bP[7]])
    s.op("vector", lambda o: o.tensor_copy(ROWS[0:1, 0:16], ps[7][0:1, 0:16]),
         reads=[bP[7]], writes=[bROWS])

    for p in range(8):
        hA, hB = 2 * p, 2 * p + 1
        high = p >= 4
        def wu_fetch(pp):
            us = (pp % 2) * 512
            for wi, dsrc in enumerate((d_wku, d_wvu)):
                s.op("sync", dma(WU[:, us + wi * 256:us + wi * 256 + 256],
                                 dsrc[:, pp * 256:(pp + 1) * 256]),
                     writes=[bWU[pp % 2]])
        if p == 0:
            wu_fetch(0)
            wu_fetch(1)
        elif p < 7:
            wu_fetch(p + 1)
        uso = (p % 2) * 512
        for wi, (dst, dstbuf) in enumerate(((KP, [bKP]), (PS1, bPS1))):
            for nb in range(4):
                bank = nb if wi == 0 else 4 + nb % 2

                def mmkv(o, wi=wi, nb=nb, bank=bank, uso=uso):
                    for kc in range(2):
                        r = o.matmul(
                            ps[bank][:],
                            WU[:, uso + wi * 256 + kc * 128:
                               uso + wi * 256 + kc * 128 + 128],
                            C[:, kc * 2048 + nb * 512:kc * 2048 + nb * 512 + 512],
                            start=(kc == 0), stop=(kc == 1))
                    return r
                s.op("tensor", mmkv, reads=[bWU[p % 2], bC],
                     writes=[bP[bank]])
                if dst is KP:
                    s.op("scalar", lambda o, nb=nb, bank=bank: o.copy(
                        KB[:, nb * 1024:nb * 1024 + 512], ps[bank][:]),
                        reads=[bP[bank]], writes=dstbuf)
                    s.op("vector", lambda o, nb=nb, bank=bank: o.tensor_sub(
                        KB[:, nb * 1024 + 512:nb * 1024 + 1024], ps[bank][:],
                        KB[:, nb * 1024:nb * 1024 + 512]),
                        reads=[bP[bank]] + dstbuf, writes=dstbuf)
                    s.op("vector", lambda o, nb=nb: o.tensor_mul(
                        EBB[:, nb * 512:(nb + 1) * 512],
                        KB[:, nb * 1024:nb * 1024 + 512],
                        KB[:, nb * 1024:nb * 1024 + 512]),
                        reads=dstbuf, writes=[bEB[nb]])
                    s.op("tensor", lambda o, nb=nb: o.matmul(
                        ps[6][0:2, :], SEGB[:, 0:2],
                        EBB[:, nb * 512:(nb + 1) * 512],
                        start=True, stop=True),
                        reads=[bEB[nb], bCONST], writes=[bP[6]])
                    s.op("vector", lambda o, nb=nb: o.tensor_copy(
                        KNS[0:2, nb * 512:(nb + 1) * 512], ps[6][0:2, :]),
                        reads=[bP[6]], writes=[bKNS])
                else:
                    s.op("vector", lambda o, dst=dst, nb=nb, bank=bank:
                         o.tensor_copy(dst[:, nb * 512:(nb + 1) * 512],
                                       ps[bank][:]),
                         reads=[bP[bank]], writes=dstbuf)
        s.op("vector", lambda o: o.reduce_max(KNS[0:2, 0:1], KNS[0:2, 0:2048],
                                              axis=AX.X),
             reads=[bKNS], writes=[bKNS])
        s.op("tensor", lambda o: o.transpose(ps[4][0:1, 0:2], KNS[0:2, 0:1],
                                             IDT[0:2, 0:2]),
             reads=[bKNS, bCONST], writes=[bP[4]])
        s.op("vector", lambda o: o.tensor_copy(ROWS[0:1, 16:18],
                                               ps[4][0:1, 0:2]),
             reads=[bP[4]], writes=[bROWS])
        s.op("vector", lambda o, p=p: o.tensor_mul(
            ROWS[0:1, 18:20], ROWS[0:1, 2 * p:2 * p + 2], ROWS[0:1, 16:18]),
            reads=[bROWS], writes=[bROWS])
        s.op("scalar", lambda o: o.activation(ROWS[0:1, 20:22],
                                              ROWS[0:1, 18:20], AF.Ln),
             reads=[bROWS], writes=[bROWS])
        s.op("scalar", lambda o: o.activation(ROWS[0:1, 22:24],
                                              ROWS[0:1, 20:22], AF.Exp,
                                              scale=0.5),
             reads=[bROWS], writes=[bROWS])
        s.op("vector", lambda o: o.tensor_scalar_mul(ROWS[0:1, 24:26],
                                                     ROWS[0:1, 22:24], -1.0),
             reads=[bROWS], writes=[bROWS])
        s.op("tensor", lambda o: o.matmul(ps[5][:, 0:2], ONE[0:1, :],
                                          ROWS[0:1, 24:26],
                                          start=True, stop=True),
             reads=[bROWS, bCONST], writes=[bP[5]])
        s.op("vector", lambda o, p=p: o.tensor_copy(
            NEGB[:, 2 * p:2 * p + 2], ps[5][:, 0:2]),
            reads=[bP[5]], writes=[bNEGB])
        for g in range(4):
            bank = 4 + (g % 2)

            def trv(o, g=g, bank=bank):
                for j in range(4):
                    r = o.transpose(
                        ps[bank][:, j * 128:(j + 1) * 128],
                        PS1[:, (g * 4 + j) * 128:(g * 4 + j + 1) * 128],
                        IDT[:])
                return r
            s.op("tensor", trv, reads=bPS1 + [bCONST], writes=[bP[bank]])
            ps3 = ps[bank].ap().rearrange("p (s c) -> p s c", c=128)

            def evv(o, g=g, bank=bank, ps3=ps3):
                o.tensor_copy(vb3[:, g * 4:(g + 1) * 4, 0:64],
                              ps3[:, :, 0:64])
                return o.tensor_copy(vb3[:, g * 4:(g + 1) * 4, 65:129],
                                     ps3[:, :, 64:128])
            s.op("vector", evv, reads=[bP[bank]], writes=[bVB])

        pend = []
        for qb in range(4):
            nkc = 4 * qb + 4
            pvbase = 4 + 2 * (qb % 2)

            def emit_s(kc, qb=qb, p=p):
                diag_i = kc - 4 * qb
                q0 = max(diag_i, 0) * 128
                sA_, sB_ = bP[kc % 2], bP[2 + kc % 2]
                pA_, pB_ = ps[kc % 2], ps[2 + kc % 2]

                def mms(o, kc=kc, qb=qb, p=p, q0=q0, pA_=pA_, pB_=pB_):
                    kb2 = (kc // 4) * 1024 + (kc % 4) * 128
                    qr2 = 2 * (p * 2048 + qb * 512)
                    for sl, pb_, rlo, rhi in ((0, pA_, 0, 64),
                                              (1, pB_, 64, 128)):
                        tp = (rlo, 0)
                        kh_ = KB[rlo:rhi, kb2:kb2 + 128]
                        kl_ = KB[rlo:rhi, kb2 + 512:kb2 + 640]
                        qh_ = BB[rlo:rhi, qr2 + q0:qr2 + 512]
                        ql_ = BB[rlo:rhi, qr2 + 512 + q0:qr2 + 1024]
                        o.matmul(pb_[:, q0:512], kh_, qh_, start=True,
                                 stop=False, tile_position=tp)
                        o.matmul(pb_[:, q0:512], kh_, ql_, start=False,
                                 stop=False, tile_position=tp)
                        r = o.matmul(pb_[:, q0:512], kl_, qh_, start=False,
                                     stop=True, tile_position=tp)
                    return r
                s.op("tensor", mms,
                     reads=[bKP] + bBs[4 * qb:4 * qb + 4], writes=[sA_, sB_])

            def emit_epv(kc, qb=qb, p=p, hA=hA, hB=hB, high=high,
                         pvbase=pvbase, nkc=nkc):
                diag_i = kc - 4 * qb
                q0 = max(diag_i, 0) * 128
                sA_, sB_ = bP[kc % 2], bP[2 + kc % 2]
                pA_, pB_ = ps[kc % 2], ps[2 + kc % 2]
                for sl, hh, sbank, pbank in ((0, hA, sA_, pA_),
                                             (1, hB, sB_, pB_)):
                    ebuf = bEB[sl * 2 + kc % 2]
                    eoff = sl * 1024 + (kc % 2) * 512
                    if diag_i < 0:
                        s.op("scalar", lambda o, eoff=eoff, pbank=pbank,
                             hh=hh: o.activation(
                                 EB[:, eoff:eoff + 512], pbank[:], AF.Exp,
                                 bias=NEGB[:, hh:hh + 1]),
                             reads=[sbank, bNEGB], writes=[ebuf])
                    else:
                        i = diag_i
                        if i < 3:
                            s.op("scalar", lambda o, eoff=eoff, pbank=pbank,
                                 hh=hh, i=i: o.activation(
                                     EB[:, eoff + (i + 1) * 128:eoff + 512],
                                     pbank[:, (i + 1) * 128:512], AF.Exp,
                                     bias=NEGB[:, hh:hh + 1]),
                                 reads=[sbank, bNEGB], writes=[ebuf])
                        s.op("scalar", lambda o, pbank=pbank, hh=hh, i=i,
                             sl=sl: o.activation(
                                 XT[:, 768 + sl * 128:768 + sl * 128 + 128],
                                 pbank[:, i * 128:(i + 1) * 128], AF.Exp,
                                 bias=NEGB[:, hh:hh + 1]),
                             reads=[sbank, bNEGB], writes=[bXT[0]])
                        s.op("vector", lambda o, eoff=eoff, i=i, sl=sl:
                             o.tensor_mul(
                                 EB[:, eoff + i * 128:eoff + (i + 1) * 128],
                                 XT[:, 768 + sl * 128:768 + sl * 128 + 128],
                                 TRI[:]),
                             reads=[bXT[0], bCONST], writes=[ebuf])
                for sl, pvbank in ((0, pvbase), (1, pvbase + 1)):
                    ebuf = bEB[sl * 2 + kc % 2]
                    eoff = sl * 1024 + (kc % 2) * 512

                    def mmpv(o, kc=kc, sl=sl, pvbank=pvbank, eoff=eoff,
                             q0=q0, nkc=nkc):
                        return o.matmul(
                            ps[pvbank][0:65, q0:512],
                            VB[:, kc * 130 + sl * 65:
                               kc * 130 + sl * 65 + 65],
                            EB[:, eoff + q0:eoff + 512],
                            start=(kc == 0), stop=(kc == nkc - 1))
                    s.op("tensor", mmpv, reads=[bVB, ebuf],
                         writes=[bP[pvbank]])

            for kc in range(nkc):
                emit_s(kc)
                if kc >= 1:
                    emit_epv(kc - 1)
                if kc == 1 and pend:
                    for fn in pend:
                        fn()
                    pend = []
            emit_epv(nkc - 1)
            def tail(qb=qb, pvbase=pvbase):
                s.op("vector", lambda o: o.tensor_copy(
                    DRS[64:65, 0:512], ps[pvbase][64:65, :]),
                    reads=[bP[pvbase]], writes=[bDRS])
                s.op("vector", lambda o: o.tensor_copy(
                    DRS[64:65, 512:1024], ps[pvbase + 1][64:65, :]),
                    reads=[bP[pvbase + 1]], writes=[bDRS])
                s.op("scalar", lambda o: o.activation(
                    DRS[64:65, 1024:2048], DRS[64:65, 0:1024],
                    AF.Ln), reads=[bDRS], writes=[bDRS])
                s.op("scalar", lambda o: o.activation(
                    DRS[64:65, 0:1024], DRS[64:65, 1024:2048],
                    AF.Exp, scale=-1.0), reads=[bDRS], writes=[bDRS])
                for sl in range(2):
                    src = DRS[64:65, sl * 512:sl * 512 + 512].rearrange(
                        "p (u n) -> p u n", u=1).broadcast_to([1, 64, 512])
                    s.op("gpsimd", lambda o, sl=sl, src=src: o.dma_start(
                        out=PS1[0:64, 1024 + sl * 512:1024 + sl * 512 + 512],
                        in_=src), reads=[bDRS], writes=[bPS1h[1]])
                for sl, hh in ((0, hA), (1, hB)):
                    hlow = hh % 8
                    if not high:
                        s.op("vector", lambda o, sl=sl, hlow=hlow,
                             pvbank=pvbase + sl, qb=qb: o.tensor_mul(
                                 A[0:64, hlow * 2048 + qb * 512:
                                   hlow * 2048 + qb * 512 + 512],
                                 ps[pvbank][0:64, :],
                                 PS1[0:64, 1024 + sl * 512:
                                     1024 + sl * 512 + 512]),
                             reads=[bP[pvbase + sl], bPS1h[1]],
                             writes=list(bAs))
                    else:
                        s.op("vector", lambda o, sl=sl,
                             pvbank=pvbase + sl: o.tensor_mul(
                                 PS1[0:64, sl * 512:sl * 512 + 512],
                                 ps[pvbank][0:64, :],
                                 PS1[0:64, 1024 + sl * 512:
                                     1024 + sl * 512 + 512]),
                             reads=[bP[pvbase + sl], bPS1h[1]],
                             writes=[bPS1h[0]])
                        s.op("gpsimd", lambda o, sl=sl, hlow=hlow, qb=qb:
                             o.dma_start(
                                 out=A[64:128, hlow * 2048 + qb * 512:
                                       hlow * 2048 + qb * 512 + 512],
                                 in_=PS1[0:64, sl * 512:sl * 512 + 512]),
                             reads=[bPS1h[0]], writes=list(bAs))
            pend.append(tail)
        for fn in pend:
            fn()
        pend = []

    oT_of = lambda kc, nb: A[:, kc * 2048 + nb * 512:kc * 2048 + nb * 512 + 512]
    for m in range(8):
        slot = m % 2
        s.op("sync", dma(WS[:, slot * 1024:slot * 1024 + 1024],
                         d_wout[:, m * 1024:(m + 1) * 1024]),
             writes=[bWS[slot]])
        proj_mblock(lambda nb, m=m: B[:, m * 2048 + nb * 512:
                                      m * 2048 + nb * 512 + 512],
                    lambda nb: bBs[4 * nb:4 * nb + 4], slot,
                    lambda nb: list(bAs), oT_of, 8)

    def ln_stats_ps(bufs, b0, b1, sl):
        mo, no = sl * 4, sl * 12
        s.op("vector", lambda o: (
            o.bn_stats(NS[:, no:no + 6], b0),
            o.bn_stats(NS[:, no + 6:no + 12], b1))[-1],
            reads=bufs, writes=[bNS[sl]])
        s.op("vector", lambda o: o.bn_aggr(MV[:, mo:mo + 2],
                                           NS[:, no:no + 12]),
             reads=[bNS[sl]], writes=[bMV[sl]])
        s.op("scalar", lambda o: o.activation(MV[:, mo + 3:mo + 4],
                                              MV[:, mo + 1:mo + 2], AF.Ln,
                                              bias=EPSC[:, 0:1], scale=1.0),
             reads=[bMV[sl], bCONST], writes=[bMV[sl]])
        s.op("scalar", lambda o: o.activation(MV[:, mo + 2:mo + 3],
                                              MV[:, mo + 3:mo + 4], AF.Exp,
                                              scale=-0.5),
             reads=[bMV[sl]], writes=[bMV[sl]])

    def p5_a(t):
        par = t % 2
        s.op("gpsimd", dma(XT[:, par * 1024:par * 1024 + 1024],
                           d_x[t * 128:(t + 1) * 128, :]), writes=[bXT[par]])
        for hf in range(2):
            bank = par * 2 + hf

            def tra(o, hf=hf, t=t, bank=bank):
                for j in range(4):
                    r = o.transpose(ps[bank][:, j * 128:(j + 1) * 128],
                                    B[:, (hf * 4 + j) * 2048 + t * 128:
                                      (hf * 4 + j) * 2048 + t * 128 + 128],
                                    IDT[:])
                return r
            s.op("tensor", tra, reads=[bBs[t], bCONST], writes=[bP[bank]])
        ln_stats_ps([bP[par * 2], bP[par * 2 + 1]],
                    ps[par * 2][:], ps[par * 2 + 1][:], par)

    def p5_b(t):
        par = t % 2
        mo = par * 4
        eoff = par * 1024
        for hf in range(2):
            bank = par * 2 + hf
            s.op("vector", lambda o, mo=mo, eoff=eoff, hf=hf, bank=bank:
                 o.tensor_scalar(
                     EB[:, eoff + hf * 512:eoff + hf * 512 + 512],
                     ps[bank][:], MV[:, mo:mo + 1], MV[:, mo + 2:mo + 3],
                     ALU.subtract, ALU.mult),
                 reads=[bP[bank], bMV[par]],
                 writes=[bEB[2 * par + hf]])
        s.op("vector", lambda o, t=t, par=par, eoff=eoff: o.tensor_add(
            A[:, t * 1024:(t + 1) * 1024], EB[:, eoff:eoff + 1024],
            XT[:, par * 1024:par * 1024 + 1024]),
            reads=[bEB[2 * par], bEB[2 * par + 1], bXT[par]],
            writes=[bAs[t]])
        ln_stats(bAs[t], A[:, t * 1024:(t + 1) * 1024], 2 + par)

    def p5_c(t):
        par = t % 2
        mo = (2 + par) * 4
        s.op("vector", lambda o, t=t, mo=mo, par=par: o.tensor_scalar(
            KP[:, par * 1024:par * 1024 + 1024],
            A[:, t * 1024:(t + 1) * 1024], MV[:, mo:mo + 1],
            MV[:, mo + 2:mo + 3], ALU.subtract, ALU.mult),
            reads=[bAs[t], bMV[2 + par]], writes=[bKP])
        tr_tile_to(lambda f, tt: B[:, f * 2048 + tt * 128:
                                   f * 2048 + tt * 128 + 128],
                   t, bKP, KP, (4 + par * 2, 5 + par * 2), [bBs[t]],
                   srcoff=par * 1024, dstbuf_is_list=True)

    def gate_mm(nb):
        def mmgate(o, nb=nb):
            for kc in range(8):
                r = o.matmul(ps[7][0:NE, :], WS[:, kc * 8:(kc + 1) * 8],
                             ln3_of(kc, nb), start=(kc == 0), stop=(kc == 7))
            return r
        s.op("tensor", mmgate, reads=[bWS[0]] + bBs[4 * nb:4 * nb + 4],
             writes=[bP[7]])
        s.op("vector", lambda o, nb=nb: o.tensor_scalar_add(
            LG[0:NE, nb * 512:(nb + 1) * 512], ps[7][0:NE, :], BG[0:NE, 0:1]),
            reads=[bP[7], bCONST], writes=[bLG])

    s.op("sync", dma(WS[:, 0:64], d_wg[:, :]), writes=[bWS[0]])
    ln3_of = lambda kc, nb: B[:, kc * 2048 + nb * 512:
                              kc * 2048 + nb * 512 + 512]
    p5_a(0)
    p5_a(1)
    p5_b(0)
    for t in range(NT):
        if t + 1 < NT:
            p5_b(t + 1)
        p5_c(t)
        if t + 2 < NT:
            p5_a(t + 2)
        if t % 4 == 3:
            gate_mm(t // 4)

    for t in range(NT):
        s.op("tensor", lambda o, t=t: o.transpose(
            ps[4][:, 0:NE], LG[0:NE, t * 128:(t + 1) * 128],
            IDT[0:NE, 0:NE]), reads=[bLG, bCONST], writes=[bP[4]])
        s.op("vector", lambda o, t=t: o.tensor_copy(
            LGT[:, t * NE:(t + 1) * NE], ps[4][:, 0:NE]),
            reads=[bP[4]], writes=[bLGT])
    for t in range(NT):
        s.op("vector", lambda o, t=t: o.reduce_max(
            GSS[:, 16 + t:17 + t], LGT[:, t * NE:(t + 1) * NE], axis=AX.X),
            reads=[bLGT], writes=[bGSS])
        s.op("vector", lambda o, t=t: o.tensor_scalar_sub(
            GES[:, t * NE:(t + 1) * NE], LGT[:, t * NE:(t + 1) * NE],
            GSS[:, 16 + t:17 + t]), reads=[bLGT, bGSS], writes=[bGES])
    s.op("scalar", lambda o: o.activation(LGT[:], GES[:], AF.Exp),
         reads=[bGES], writes=[bLGT])
    for t in range(NT):
        s.op("vector", lambda o, t=t: o.reduce_sum(
            GSS[:, t:t + 1], LGT[:, t * NE:(t + 1) * NE], axis=AX.X),
            reads=[bLGT], writes=[bGSS])
    s.op("scalar", lambda o: o.activation(GSS[:, 16:32], GSS[:, 0:16], AF.Ln),
         reads=[bGSS], writes=[bGSS])
    s.op("scalar", lambda o: o.activation(GSS[:, 0:16], GSS[:, 16:32],
                                          AF.Exp, scale=-1.0),
         reads=[bGSS], writes=[bGSS])
    for t in range(NT):
        s.op("tensor", lambda o, t=t: o.transpose(
            ps[4][0:1, 0:128], GSS[:, 0 + t:1 + t], IDT[:]),
            reads=[bGSS, bCONST], writes=[bP[4]])
        s.op("vector", lambda o, t=t: o.tensor_copy(
            BPROW[0:1, t * 128:(t + 1) * 128], ps[4][0:1, 0:128]),
            reads=[bP[4]], writes=[bBPROW])
    for nb in range(4):
        s.op("tensor", lambda o, nb=nb: o.matmul(
            ps[5][:], ONE[0:1, :], BPROW[0:1, nb * 512:(nb + 1) * 512],
            start=True, stop=True), reads=[bBPROW, bCONST], writes=[bP[5]])
        s.op("vector", lambda o, nb=nb: o.tensor_copy(
            KP[:, nb * 512:(nb + 1) * 512], ps[5][:]),
            reads=[bP[5]], writes=[bKP])
    for s4 in range(4):
        for mf in range(8):
            slot = mf % 2
            s.op("sync", dma(WS[:, slot * 1024:slot * 1024 + 1024],
                             d_we[:, mf * 1024:(mf + 1) * 1024]),
                 writes=[bWS[slot]])
            bank = 4 + mf % 4

            def mme(o, mf=mf, s4=s4, slot=slot, bank=bank):
                for kc in range(8):
                    r = o.matmul(ps[bank][:],
                                 WS[:, slot * 1024 + kc * 128:
                                    slot * 1024 + kc * 128 + 128],
                                 ln3_of(kc, s4), start=(kc == 0),
                                 stop=(kc == 7))
                return r
            s.op("tensor", mme, reads=[bWS[slot]] + bBs[4 * s4:4 * s4 + 4],
                 writes=[bP[bank]])
            s.op("vector", lambda o, mf=mf, s4=s4, bank=bank:
                 o.scalar_tensor_tensor(
                     C[:, mf * 512:(mf + 1) * 512], ps[bank][:],
                     BE[:, mf:mf + 1], KP[:, s4 * 512:(s4 + 1) * 512],
                     ALU.add, ALU.mult),
                 reads=[bP[bank], bCONST, bKP], writes=[bC])
        def p6_a(tt, s4=s4):
            t = s4 * 4 + tt
            par = t % 2
            for hf in range(2):
                bank = par * 2 + hf

                def trf(o, hf=hf, tt=tt, bank=bank):
                    for j in range(4):
                        f = hf * 4 + j
                        r = o.transpose(ps[bank][:, j * 128:(j + 1) * 128],
                                        C[:, f * 512 + tt * 128:
                                          f * 512 + tt * 128 + 128], IDT[:])
                    return r
                s.op("tensor", trf, reads=[bC, bCONST], writes=[bP[bank]])
            ln_stats_ps([bP[par * 2], bP[par * 2 + 1]],
                        ps[par * 2][:], ps[par * 2 + 1][:], par)

        def p6_b(tt, s4=s4):
            t = s4 * 4 + tt
            par = t % 2
            mo = par * 4
            for hf in range(2):
                bank = par * 2 + hf
                s.op("vector", lambda o, mo=mo, par=par, hf=hf, bank=bank:
                     o.tensor_scalar(
                         EB[:, par * 1024 + hf * 512:
                            par * 1024 + hf * 512 + 512],
                         ps[bank][:], MV[:, mo:mo + 1], MV[:, mo + 2:mo + 3],
                         ALU.subtract, ALU.mult),
                     reads=[bP[bank], bMV[par]],
                     writes=[bEB[2 * par + hf]])
            s.op("vector", lambda o, t=t, par=par: o.tensor_add(
                XT[:, par * 1024:par * 1024 + 1024],
                EB[:, par * 1024:par * 1024 + 1024],
                A[:, t * 1024:(t + 1) * 1024]),
                reads=[bEB[2 * par], bEB[2 * par + 1], bAs[t]],
                writes=[bXT[par]])
            s.op("gpsimd", lambda o, t=t, par=par: o.dma_start(
                out=d_y[t * 128:(t + 1) * 128, :],
                in_=XT[:, par * 1024:par * 1024 + 1024]),
                reads=[bXT[par]], writes=[bY])

        p6_a(0)
        p6_a(1)
        for tt in range(4):
            p6_b(tt)
            if tt + 2 < 4:
                p6_a(tt + 2)

    for i in range(NDMASEM):
        key = f"gpsimd{i}"
        fv = s.count[key]
        if fv:
            s.raw("gpsimd", lambda o, key=key, fv=fv: o.wait_ge(
                s.sems[key], fv))

    block = es.enter_context(nc.Block())
    with es:
        for ename in ENGINES:
            getattr(block, ename)(
                (lambda prog: lambda obj: [f(obj) for f in prog])(
                    s.prog[ename]))
    return nc


_NC_CACHE = []


def prepare_inputs(L0_attn_x_bld, L0_attn_Wq, L0_attn_Wkv_down,
                   L0_attn_Wk_up, L0_attn_Wv_up, L0_attn_Wout, L0_ffn_Wg,
                   L0_ffn_bg, L0_ffn_We, L0_ffn_be):
    def streamed(w, nm):
        k = w.shape[0]
        return np.ascontiguousarray(
            w.reshape(k // 128, 128, nm, 128).transpose(1, 2, 0, 3)
            .reshape(128, nm * (k // 128) * 128))

    x = np.ascontiguousarray(np.asarray(L0_attn_x_bld, dtype=np.float32))
    wq = streamed(np.asarray(L0_attn_Wq, dtype=np.float32).reshape(DM, DM), 8)
    wkv = streamed(np.asarray(L0_attn_Wkv_down, dtype=np.float32), 2)
    wku = streamed(np.asarray(L0_attn_Wk_up,
                              dtype=np.float32).reshape(LD, DM), 8)
    wvu = streamed(np.asarray(L0_attn_Wv_up,
                              dtype=np.float32).reshape(LD, DM), 8)
    wout0 = np.asarray(L0_attn_Wout, dtype=np.float32).reshape(DM, DM)
    wout = streamed(np.ascontiguousarray(np.concatenate(
        [np.concatenate([wout0[h * 64:(h + 1) * 64],
                         wout0[(h + 8) * 64:(h + 9) * 64]], axis=0)
         for h in range(8)], axis=0)), 8)
    wg = np.ascontiguousarray(
        np.asarray(L0_ffn_Wg, dtype=np.float32).reshape(8, 128, NE)
        .transpose(1, 0, 2).reshape(128, 64))
    bgc = np.ascontiguousarray(np.asarray(L0_ffn_bg,
                                          dtype=np.float32).reshape(NE, 1))
    we = streamed(np.asarray(L0_ffn_We, dtype=np.float32), 8)
    bet = np.ascontiguousarray(np.asarray(L0_ffn_be,
                                          dtype=np.float32).reshape(NFC,
                                                                    128).T)
    cid = np.eye(128, dtype=np.float32)
    cones = np.ones((128, 128), dtype=np.float32)
    ctri = np.triu(np.ones((128, 128), dtype=np.float32))
    cseg = np.zeros((128, 2), dtype=np.float32)
    cseg[0:64, 0] = 1.0
    cseg[64:128, 1] = 1.0
    import ml_dtypes
    csegs = np.zeros((128, 128), dtype=ml_dtypes.bfloat16)
    for m in range(8):
        csegs[0:64, m * 16 + 2 * m] = 1.0
        csegs[64:128, m * 16 + 2 * m + 1] = 1.0
    csegb = np.ascontiguousarray(cseg.astype(ml_dtypes.bfloat16))

    shared = {"wq": wq, "wkv": wkv, "wku": wku, "wvu": wvu, "wout": wout,
              "wg": wg, "bgc": bgc, "we": we, "bet": bet, "cid": cid,
              "cones": cones, "ctri": ctri, "cseg": cseg, "csegs": csegs, "csegb": csegb}
    return shared, x


def kernel(**inputs):
    shared, x = prepare_inputs(**inputs)
    if not _NC_CACHE:
        _NC_CACHE.append(build_nc())
    nc = _NC_CACHE[0]
    in_maps = [dict(shared, x=np.ascontiguousarray(x[b])) for b in range(8)]
    res = run_bass_kernel_spmd(nc, in_maps, core_ids=list(range(8)))
    out = np.stack([np.asarray(r["y"], dtype=np.float32)
                    for r in res.results], axis=0)
    return out


# version 22
# speedup vs baseline: 1.0679x; 1.0678x over previous
import numpy as np
from contextlib import ExitStack

import concourse.bass as bass
import concourse.mybir as mybir
from concourse.bass_utils import run_bass_kernel_spmd

F32 = mybir.dt.float32
BF16 = mybir.dt.bfloat16
AF = mybir.ActivationFunctionType
ALU = mybir.AluOpType
AX = mybir.AxisListType

L, DM, H, HD, LD, NE = 2048, 1024, 16, 64, 256, 8
NT = L // 128
NFC = DM // 128
LNEPS = 1e-5
DBG, DBG_T = False, -1

ENGINES = ("tensor", "vector", "scalar", "sync", "gpsimd")
DMA_ENGINES = ("sync", "gpsimd")
NDMASEM = 4
SEMKEYS = ["tensor", "vector", "scalar"] + [
    f"{e}{i}" for e in DMA_ENGINES for i in range(NDMASEM)]


class Buf:
    __slots__ = ("name", "w", "r")

    def __init__(self, name):
        self.name = name
        self.w = {}
        self.r = {}


class Sched:
    def __init__(self):
        self.prog = {e: [] for e in ENGINES}
        self.count = {k: 0 for k in SEMKEYS}
        self.bound = {e: {k: 0 for k in SEMKEYS} for e in ENGINES}
        self.dma_rr = {e: 0 for e in DMA_ENGINES}
        self.sems = {}

    def op(self, eng, build, reads=(), writes=()):
        deps = {}
        for b in reads:
            for we, wv in b.w.items():
                deps[we] = max(deps.get(we, 0), wv)
        for b in writes:
            for we, wv in b.w.items():
                deps[we] = max(deps.get(we, 0), wv)
            for re, rv in b.r.items():
                deps[re] = max(deps.get(re, 0), rv)
        if eng in DMA_ENGINES:
            key = f"{eng}{self.dma_rr[eng] % NDMASEM}"
            self.dma_rr[eng] += 1
            step = 16
            deps[key] = max(deps.get(key, 0), self.count[key])
        else:
            key, step = eng, 1
        waits = []
        for de, dv in deps.items():
            if self.bound[eng][de] >= dv:
                continue
            waits.append((de, dv))
            self.bound[eng][de] = dv
        self.count[key] += step
        v = self.count[key]
        sems = self.sems

        def thunk(obj, _waits=tuple(waits), _build=build, _key=key,
                  _step=step):
            for de, dv in _waits:
                obj.wait_ge(sems[de], dv)
            last = _build(obj)
            last.then_inc(sems[_key], _step)

        self.prog[eng].append(thunk)
        for b in writes:
            b.w = {key: v}
            b.r = {}
        for b in reads:
            b.r[key] = max(b.r.get(key, 0), v)
        return v

    def raw(self, eng, build):
        self.prog[eng].append(lambda obj, _b=build: _b(obj))


def build_nc():
    nc = bass.Bass("TRN2", target_bir_lowering=False)
    ap = lambda t: t.ap()
    d_x = ap(nc.dram_tensor("x", [L, DM], F32, kind="ExternalInput"))
    d_wq = ap(nc.dram_tensor("wq", [128, 8192], F32, kind="ExternalInput"))
    d_wkv = ap(nc.dram_tensor("wkv", [128, 2048], F32, kind="ExternalInput"))
    d_wku = ap(nc.dram_tensor("wku", [128, 2048], F32, kind="ExternalInput"))
    d_wvu = ap(nc.dram_tensor("wvu", [128, 2048], F32, kind="ExternalInput"))
    d_wout = ap(nc.dram_tensor("wout", [128, 8192], F32, kind="ExternalInput"))
    d_wg = ap(nc.dram_tensor("wg", [128, 64], F32, kind="ExternalInput"))
    d_bg = ap(nc.dram_tensor("bgc", [NE, 1], F32, kind="ExternalInput"))
    d_we = ap(nc.dram_tensor("we", [128, 8192], F32, kind="ExternalInput"))
    d_bet = ap(nc.dram_tensor("bet", [128, NFC], F32, kind="ExternalInput"))
    d_id = ap(nc.dram_tensor("cid", [128, 128], F32, kind="ExternalInput"))
    d_one = ap(nc.dram_tensor("cones", [128, 128], F32, kind="ExternalInput"))
    d_tri = ap(nc.dram_tensor("ctri", [128, 128], F32, kind="ExternalInput"))
    d_seg = ap(nc.dram_tensor("cseg", [128, 2], F32, kind="ExternalInput"))
    d_segs = ap(nc.dram_tensor("csegs", [128, 128], BF16,
                               kind="ExternalInput"))
    d_segb = ap(nc.dram_tensor("csegb", [128, 2], BF16,
                               kind="ExternalInput"))
    d_y = ap(nc.dram_tensor("y", [L, DM], F32, kind="ExternalOutput"))
    if DBG:
        d_dbg = ap(nc.dram_tensor("dbg", [128, 4096], F32,
                                  kind="ExternalOutput"))

    es = ExitStack()
    sb = lambda name, fr: es.enter_context(
        nc.sbuf_tensor(name, [128, fr], F32))
    A = sb("A", 16384)
    B = sb("B", 16384)
    C = sb("C", 4096)
    KP = sb("KP", 2048)
    PS1 = sb("PS1", 2048)
    VB = sb("VB", 2080)
    EB = sb("EB", 2048)
    WS = sb("WS", 2048)
    WU = sb("WU", 1024)
    XT = sb("XT", 2048)
    SCR = sb("SCR", 2048)
    IDT = sb("IDT", 128)
    ONE = sb("ONE", 128)
    TRI = sb("TRI", 128)
    SEG = sb("SEG", 2)
    SEGS = es.enter_context(nc.sbuf_tensor("SEGS", [128, 128], BF16))
    SEGB = es.enter_context(nc.sbuf_tensor("SEGB", [128, 2], BF16))
    BE = sb("BE", NFC)
    BG = sb("BG", 1)
    ROWS = sb("ROWS", 32)
    NEGB = sb("NEGB", 16)
    MV = sb("MV", 24)
    EPSC = sb("EPSC", 1)
    NS = sb("NS", 48)
    LGT = sb("LGT", 128)
    GES = sb("GES", 128)
    GSS = sb("GSS", 32)
    QN = KNS = DRS = LG = BPROW = SCR

    ps = [es.enter_context(nc.psum_tensor(f"p{i}", [128, 512], F32))
          for i in range(8)]

    PS1B = nc.alloc_sbuf_tensor_at("PS1B", [128, 4096], BF16,
                                   offset=nc.lookup_mloc(PS1).addr)
    KB = nc.alloc_sbuf_tensor_at("KB", [128, 4096], BF16,
                                 offset=nc.lookup_mloc(KP).addr)
    BB = nc.alloc_sbuf_tensor_at("BB", [128, 32768], BF16,
                                 offset=nc.lookup_mloc(B).addr)
    EBB = nc.alloc_sbuf_tensor_at("EBB", [128, 4096], BF16,
                                  offset=nc.lookup_mloc(EB).addr)
    s = Sched()
    for k in SEMKEYS:
        s.sems[k] = es.enter_context(nc.semaphore(f"sem_{k}"))

    bC, bKP, bVB = (Buf(n) for n in "C KP VB".split())
    bPS1h = [Buf("PS1a"), Buf("PS1b")]
    bPS1 = bPS1h
    bAs = [Buf(f"A{t}") for t in range(NT)]
    bBs = [Buf(f"B{t}") for t in range(NT)]
    bEB = [Buf(f"EB{i}") for i in range(4)]
    bWS = [Buf("WS0"), Buf("WS1")]
    bWU = [Buf("WU0"), Buf("WU1")]
    bXT = [Buf("XT0"), Buf("XT1")]
    bP = [Buf(f"p{i}") for i in range(8)]
    bCONST = Buf("CONST")
    bSCR = Buf("SCR")
    bQN = bKNS = bDRS = bLG = bBPROW = bSCR
    bROWS, bNEGB = Buf("ROWS"), Buf("NEGB")
    bMV = [Buf(f"MV{i}") for i in range(6)]
    bNS = [Buf(f"NS{i}") for i in range(6)]
    bLGT, bGES, bGSS, bDSCR = (Buf(n) for n in "LGT GES GSS DSCR".split())
    bDBG = Buf("DBG")
    bY = Buf("Y")

    def dma(out, in_):
        return lambda o: o.dma_start(out=out, in_=in_)

    s.op("sync", dma(IDT[:], d_id[:, :]), writes=[bCONST])
    s.op("sync", dma(ONE[:], d_one[:, :]), writes=[bCONST])
    s.op("sync", dma(TRI[:], d_tri[:, :]), writes=[bCONST])
    s.op("sync", dma(SEG[:], d_seg[:, :]), writes=[bCONST])
    s.op("sync", dma(SEGS[:], d_segs[:, :]), writes=[bCONST])
    s.op("sync", dma(SEGB[:], d_segb[:, :]), writes=[bCONST])
    s.op("vector", lambda o: o.memset(EPSC[:], LNEPS), writes=[bCONST])
    s.op("sync", dma(BE[:], d_bet[:, :]), writes=[bCONST])
    s.op("vector", lambda o: o.memset(VB[:], 0.0), writes=[bVB])
    vb3 = VB.ap().rearrange("p (s c) -> p s c", c=130)
    s.op("vector", lambda o: o.memset(vb3[:, :, 64:65], 1.0), writes=[bVB])
    s.op("vector", lambda o: o.memset(vb3[:, :, 129:130], 1.0), writes=[bVB])
    s.op("sync", dma(BG[0:NE, :], d_bg[:, :]), writes=[bCONST])

    def ln_stats(xbuf, xslice, par=0):
        mo, no = par * 4, (par % 4) * 12
        s.op("vector", lambda o: (
            o.bn_stats(NS[:, no:no + 6], xslice[:, 0:512]),
            o.bn_stats(NS[:, no + 6:no + 12], xslice[:, 512:1024]))[-1],
            reads=[xbuf], writes=[bNS[par % 4]])
        s.op("vector", lambda o: o.bn_aggr(MV[:, mo:mo + 2],
                                           NS[:, no:no + 12]),
             reads=[bNS[par % 4]], writes=[bMV[par]])
        s.op("scalar", lambda o: o.activation(MV[:, mo + 3:mo + 4],
                                              MV[:, mo + 1:mo + 2], AF.Ln,
                                              bias=EPSC[:, 0:1], scale=1.0),
             reads=[bMV[par], bCONST], writes=[bMV[par]])
        s.op("scalar", lambda o: o.activation(MV[:, mo + 2:mo + 3],
                                              MV[:, mo + 3:mo + 4], AF.Exp,
                                              scale=-0.5),
             reads=[bMV[par]], writes=[bMV[par]])

    def tr_tile_to(dstT, t, srcbuf, srctensor, banks, dstbuf, srcoff=0,
                   dstbuf_is_list=False):
        src = srctensor if srctensor is not None else PS1
        for hf in range(2):
            bank = banks[hf]

            def trf(o, hf=hf, bank=bank):
                for j in range(4):
                    r = o.transpose(
                        ps[bank][:, j * 128:(j + 1) * 128],
                        src[:, srcoff + (hf * 4 + j) * 128:
                            srcoff + (hf * 4 + j + 1) * 128],
                        IDT[:])
                return r
            s.op("tensor", trf, reads=[srcbuf, bCONST], writes=[bP[bank]])

            def evf(o, hf=hf, bank=bank, t=t):
                for j in range(4):
                    f = hf * 4 + j
                    r = o.copy(dstT(f, t), ps[bank][:, j * 128:(j + 1) * 128])
                return r
            s.op("scalar", evf, reads=[bP[bank]],
                 writes=dstbuf if dstbuf_is_list else [dstbuf])

    def p1_a(t):
        par = t % 2
        xs_ = XT[:, par * 1024:par * 1024 + 1024]
        s.op("gpsimd", dma(xs_, d_x[t * 128:(t + 1) * 128, :]),
             writes=[bXT[par]])
        ln_stats(bXT[par], xs_, par)

    def p1_b(t):
        par = t % 2
        xs_ = XT[:, par * 1024:par * 1024 + 1024]
        wk_ = PS1[:, par * 1024:par * 1024 + 1024]
        mo = par * 4
        s.op("vector", lambda o, xs_=xs_, wk_=wk_, mo=mo: o.tensor_scalar(
            wk_, xs_, MV[:, mo:mo + 1], MV[:, mo + 2:mo + 3],
            ALU.subtract, ALU.mult),
            reads=[bXT[par], bMV[par]], writes=[bPS1h[par]])
        tr_tile_to(lambda f, tt: A[:, f * 2048 + tt * 128:
                                   f * 2048 + tt * 128 + 128],
                   t, bPS1h[par], PS1, (par * 2, par * 2 + 1),
                   [bAs[t]], srcoff=par * 1024, dstbuf_is_list=True)

    def proj_group(dst, dstbufs, wslot, rhsbufs, rhs_of, kchunks, nb, bank,
                   scale=None, evac=None):
        def mmg(o, nb=nb, bank=bank):
            for kc in range(kchunks):
                r = o.matmul(
                    ps[bank][:],
                    WS[:, wslot * 1024 + kc * 128:
                       wslot * 1024 + kc * 128 + 128],
                    rhs_of(kc, nb),
                    start=(kc == 0), stop=(kc == kchunks - 1))
            return r
        s.op("tensor", mmg, reads=[bWS[wslot]] + rhsbufs(nb),
             writes=[bP[bank]])
        if evac is not None:
            evac(bank)
        elif scale is None:
            s.op("scalar", lambda o, nb=nb, bank=bank: o.copy(
                dst(nb), ps[bank][:]), reads=[bP[bank]],
                writes=dstbufs(nb))
        else:
            s.op("scalar", lambda o, nb=nb, bank=bank: o.activation(
                dst(nb), ps[bank][:], AF.Copy, scale=scale),
                reads=[bP[bank]], writes=dstbufs(nb))

    def proj_mblock(dst, dstbufs, wslot, rhsbufs, rhs_of, kchunks, scale=None,
                    banks=(0, 1, 2, 3)):
        for nb in range(4):
            proj_group(dst, dstbufs, wslot, rhsbufs, rhs_of, kchunks, nb,
                       banks[nb % len(banks)], scale)

    lnT_of = lambda kc, nb: A[:, kc * 2048 + nb * 512:
                              kc * 2048 + nb * 512 + 512]

    def p2_part(nb):
        for m in range(2):
            slot = m % 2
            s.op("sync", dma(WS[:, slot * 1024:slot * 1024 + 1024],
                             d_wkv[:, m * 1024:(m + 1) * 1024]),
                 writes=[bWS[slot]])
            proj_group(lambda nb, m=m: C[:, m * 2048 + nb * 512:
                                         m * 2048 + nb * 512 + 512],
                       lambda nb: [bC], slot,
                       lambda nb: bAs[4 * nb:4 * nb + 4], lnT_of, 8, nb,
                       4 + m % 4)
        for m in range(8):
            slot = m % 2
            s.op("sync", dma(WS[:, slot * 1024:slot * 1024 + 1024],
                             d_wq[:, m * 1024:(m + 1) * 1024]),
                 writes=[bWS[slot]])
            bank = 4 + (m + 2) % 4

            def evq(bank, m=m, nb=nb):
                r2 = 2 * (m * 2048 + nb * 512)
                qh = BB[:, r2:r2 + 512]
                ql = BB[:, r2 + 512:r2 + 1024]
                dd = bBs[4 * nb:4 * nb + 4]
                s.op("scalar", lambda o, bank=bank, qh=qh: o.activation(
                    qh, ps[bank][:], AF.Copy, scale=0.125),
                    reads=[bP[bank]], writes=dd)
                s.op("vector", lambda o, bank=bank, qh=qh, ql=ql:
                     o.scalar_tensor_tensor(ql, ps[bank][:], 0.125, qh,
                                            ALU.mult, ALU.subtract),
                     reads=[bP[bank]] + dd, writes=dd)
            proj_group(None, None, slot,
                       lambda nb: bAs[4 * nb:4 * nb + 4], lnT_of, 8, nb,
                       bank, evac=evq)

    p1_a(0)
    for t in range(NT):
        p1_b(t)
        if t + 1 < NT:
            p1_a(t + 1)
        if t % 4 == 3:
            p2_part(t // 4)

    for m in range(8):
        def qsq(o, m=m):
            for nbq in range(4):
                r2 = 2 * (m * 2048 + nbq * 512)
                r = o.tensor_mul(PS1B[:, nbq * 512:(nbq + 1) * 512],
                                 BB[:, r2:r2 + 512], BB[:, r2:r2 + 512])
            return r
        s.op("vector", qsq, reads=list(bBs), writes=bPS1)
        for nb in range(4):
            s.op("tensor", lambda o, m=m, nb=nb: o.matmul(
                ps[4 + nb][0:16, :], SEGS[:, m * 16:(m + 1) * 16],
                PS1B[:, nb * 512:(nb + 1) * 512],
                start=(m == 0), stop=(m == 7)),
                reads=bPS1 + [bCONST], writes=[bP[4 + nb]])
    for nb in range(4):
        s.op("vector", lambda o, nb=nb: o.tensor_copy(
            QN[0:16, nb * 512:(nb + 1) * 512], ps[4 + nb][0:16, :]),
            reads=[bP[4 + nb]], writes=[bQN])
    s.op("vector", lambda o: o.reduce_max(QN[0:16, 0:1], QN[0:16, 0:2048],
                                          axis=AX.X),
         reads=[bQN], writes=[bQN])
    s.op("tensor", lambda o: o.transpose(ps[7][0:1, 0:16], QN[0:16, 0:1],
                                         IDT[0:16, 0:16]),
         reads=[bQN, bCONST], writes=[bP[7]])
    s.op("vector", lambda o: o.tensor_copy(ROWS[0:1, 0:16], ps[7][0:1, 0:16]),
         reads=[bP[7]], writes=[bROWS])

    pend = []
    for p in range(8):
        hA, hB = 2 * p, 2 * p + 1
        high = p >= 4
        def wu_fetch(pp):
            us = (pp % 2) * 512
            for wi, dsrc in enumerate((d_wku, d_wvu)):
                s.op("sync", dma(WU[:, us + wi * 256:us + wi * 256 + 256],
                                 dsrc[:, pp * 256:(pp + 1) * 256]),
                     writes=[bWU[pp % 2]])
        if p == 0:
            wu_fetch(0)
            wu_fetch(1)
        elif p < 7:
            wu_fetch(p + 1)
        else:
            for m in range(2):
                s.op("sync", dma(WS[:, m * 1024:(m + 1) * 1024],
                                 d_wout[:, m * 1024:(m + 1) * 1024]),
                     writes=[bWS[m]])
        uso = (p % 2) * 512
        for wi, (dst, dstbuf) in enumerate(((KP, [bKP]), (PS1, bPS1))):
            for nb in range(4):
                bank = nb

                def mmkv(o, wi=wi, nb=nb, bank=bank, uso=uso):
                    for kc in range(2):
                        r = o.matmul(
                            ps[bank][:],
                            WU[:, uso + wi * 256 + kc * 128:
                               uso + wi * 256 + kc * 128 + 128],
                            C[:, kc * 2048 + nb * 512:kc * 2048 + nb * 512 + 512],
                            start=(kc == 0), stop=(kc == 1))
                    return r
                s.op("tensor", mmkv, reads=[bWU[p % 2], bC],
                     writes=[bP[bank]])
                if dst is KP:
                    s.op("scalar", lambda o, nb=nb, bank=bank: o.copy(
                        KB[:, nb * 1024:nb * 1024 + 512], ps[bank][:]),
                        reads=[bP[bank]], writes=dstbuf)
                    s.op("vector", lambda o, nb=nb, bank=bank: o.tensor_sub(
                        KB[:, nb * 1024 + 512:nb * 1024 + 1024], ps[bank][:],
                        KB[:, nb * 1024:nb * 1024 + 512]),
                        reads=[bP[bank]] + dstbuf, writes=dstbuf)
                else:
                    s.op("vector", lambda o, dst=dst, nb=nb, bank=bank:
                         o.tensor_copy(dst[:, nb * 512:(nb + 1) * 512],
                                       ps[bank][:]),
                         reads=[bP[bank]], writes=dstbuf)
        def ksq(o):
            for nbq in range(4):
                r = o.tensor_mul(EBB[:, nbq * 512:(nbq + 1) * 512],
                                 KB[:, nbq * 1024:nbq * 1024 + 512],
                                 KB[:, nbq * 1024:nbq * 1024 + 512])
            return r
        s.op("vector", ksq, reads=[bKP], writes=list(bEB))
        for nb in range(4):
            s.op("tensor", lambda o, nb=nb: o.matmul(
                ps[4][0:2, :], SEGB[:, 0:2], EBB[:, nb * 512:(nb + 1) * 512],
                start=True, stop=True),
                reads=[bEB[0], bEB[1], bEB[2], bEB[3], bCONST],
                writes=[bP[4]])
            s.op("vector", lambda o, nb=nb: o.tensor_copy(
                KNS[0:2, nb * 512:(nb + 1) * 512], ps[4][0:2, :]),
                reads=[bP[4]], writes=[bKNS])
        s.op("vector", lambda o: o.reduce_max(KNS[0:2, 0:1], KNS[0:2, 0:2048],
                                              axis=AX.X),
             reads=[bKNS], writes=[bKNS])
        s.op("tensor", lambda o: o.transpose(ps[4][0:1, 0:2], KNS[0:2, 0:1],
                                             IDT[0:2, 0:2]),
             reads=[bKNS, bCONST], writes=[bP[4]])
        s.op("vector", lambda o: o.tensor_copy(ROWS[0:1, 16:18],
                                               ps[4][0:1, 0:2]),
             reads=[bP[4]], writes=[bROWS])
        s.op("vector", lambda o, p=p: o.tensor_mul(
            ROWS[0:1, 18:20], ROWS[0:1, 2 * p:2 * p + 2], ROWS[0:1, 16:18]),
            reads=[bROWS], writes=[bROWS])
        s.op("scalar", lambda o: o.activation(ROWS[0:1, 20:22],
                                              ROWS[0:1, 18:20], AF.Ln),
             reads=[bROWS], writes=[bROWS])
        s.op("scalar", lambda o: o.activation(ROWS[0:1, 22:24],
                                              ROWS[0:1, 20:22], AF.Exp,
                                              scale=0.5),
             reads=[bROWS], writes=[bROWS])
        s.op("vector", lambda o: o.tensor_scalar_mul(ROWS[0:1, 24:26],
                                                     ROWS[0:1, 22:24], -1.0),
             reads=[bROWS], writes=[bROWS])
        s.op("tensor", lambda o: o.matmul(ps[5][:, 0:2], ONE[0:1, :],
                                          ROWS[0:1, 24:26],
                                          start=True, stop=True),
             reads=[bROWS, bCONST], writes=[bP[5]])
        s.op("vector", lambda o, p=p: o.tensor_copy(
            NEGB[:, 2 * p:2 * p + 2], ps[5][:, 0:2]),
            reads=[bP[5]], writes=[bNEGB])
        for g in range(4):
            bank = 4 + (g % 2)

            def trv(o, g=g, bank=bank):
                for j in range(4):
                    r = o.transpose(
                        ps[bank][:, j * 128:(j + 1) * 128],
                        PS1[:, (g * 4 + j) * 128:(g * 4 + j + 1) * 128],
                        IDT[:])
                return r
            s.op("tensor", trv, reads=bPS1 + [bCONST], writes=[bP[bank]])
            ps3 = ps[bank].ap().rearrange("p (s c) -> p s c", c=128)

            def evv(o, g=g, bank=bank, ps3=ps3):
                o.tensor_copy(vb3[:, g * 4:(g + 1) * 4, 0:64],
                              ps3[:, :, 0:64])
                return o.tensor_copy(vb3[:, g * 4:(g + 1) * 4, 65:129],
                                     ps3[:, :, 64:128])
            s.op("vector", evv, reads=[bP[bank]], writes=[bVB])

        for qb in range(4):
            nkc = 4 * qb + 4
            pvbase = 4 + 2 * (qb % 2)

            def emit_s(kc, qb=qb, p=p):
                diag_i = kc - 4 * qb
                q0 = max(diag_i, 0) * 128
                sA_, sB_ = bP[kc % 2], bP[2 + kc % 2]
                pA_, pB_ = ps[kc % 2], ps[2 + kc % 2]

                def mms(o, kc=kc, qb=qb, p=p, q0=q0, pA_=pA_, pB_=pB_):
                    kb2 = (kc // 4) * 1024 + (kc % 4) * 128
                    qr2 = 2 * (p * 2048 + qb * 512)
                    for sl, pb_, rlo, rhi in ((0, pA_, 0, 64),
                                              (1, pB_, 64, 128)):
                        tp = (rlo, 0)
                        kh_ = KB[rlo:rhi, kb2:kb2 + 128]
                        kl_ = KB[rlo:rhi, kb2 + 512:kb2 + 640]
                        qh_ = BB[rlo:rhi, qr2 + q0:qr2 + 512]
                        ql_ = BB[rlo:rhi, qr2 + 512 + q0:qr2 + 1024]
                        o.matmul(pb_[:, q0:512], kh_, qh_, start=True,
                                 stop=False, tile_position=tp)
                        o.matmul(pb_[:, q0:512], kh_, ql_, start=False,
                                 stop=False, tile_position=tp)
                        r = o.matmul(pb_[:, q0:512], kl_, qh_, start=False,
                                     stop=True, tile_position=tp)
                    return r
                s.op("tensor", mms,
                     reads=[bKP] + bBs[4 * qb:4 * qb + 4], writes=[sA_, sB_])

            def emit_epv(kc, qb=qb, p=p, hA=hA, hB=hB, high=high,
                         pvbase=pvbase, nkc=nkc):
                diag_i = kc - 4 * qb
                q0 = max(diag_i, 0) * 128
                sA_, sB_ = bP[kc % 2], bP[2 + kc % 2]
                pA_, pB_ = ps[kc % 2], ps[2 + kc % 2]
                for sl, hh, sbank, pbank in ((0, hA, sA_, pA_),
                                             (1, hB, sB_, pB_)):
                    ebuf = bEB[sl * 2 + kc % 2]
                    eoff = sl * 1024 + (kc % 2) * 512
                    if diag_i < 0:
                        s.op("scalar", lambda o, eoff=eoff, pbank=pbank,
                             hh=hh: o.activation(
                                 EB[:, eoff:eoff + 512], pbank[:], AF.Exp,
                                 bias=NEGB[:, hh:hh + 1]),
                             reads=[sbank, bNEGB], writes=[ebuf])
                    else:
                        i = diag_i
                        if i < 3:
                            s.op("scalar", lambda o, eoff=eoff, pbank=pbank,
                                 hh=hh, i=i: o.activation(
                                     EB[:, eoff + (i + 1) * 128:eoff + 512],
                                     pbank[:, (i + 1) * 128:512], AF.Exp,
                                     bias=NEGB[:, hh:hh + 1]),
                                 reads=[sbank, bNEGB], writes=[ebuf])
                        s.op("scalar", lambda o, pbank=pbank, hh=hh, i=i,
                             sl=sl: o.activation(
                                 XT[:, 768 + sl * 128:768 + sl * 128 + 128],
                                 pbank[:, i * 128:(i + 1) * 128], AF.Exp,
                                 bias=NEGB[:, hh:hh + 1]),
                             reads=[sbank, bNEGB], writes=[bXT[0]])
                        s.op("vector", lambda o, eoff=eoff, i=i, sl=sl:
                             o.tensor_mul(
                                 EB[:, eoff + i * 128:eoff + (i + 1) * 128],
                                 XT[:, 768 + sl * 128:768 + sl * 128 + 128],
                                 TRI[:]),
                             reads=[bXT[0], bCONST], writes=[ebuf])
                for sl, pvbank in ((0, pvbase), (1, pvbase + 1)):
                    ebuf = bEB[sl * 2 + kc % 2]
                    eoff = sl * 1024 + (kc % 2) * 512

                    def mmpv(o, kc=kc, sl=sl, pvbank=pvbank, eoff=eoff,
                             q0=q0, nkc=nkc):
                        return o.matmul(
                            ps[pvbank][0:65, q0:512],
                            VB[:, kc * 130 + sl * 65:
                               kc * 130 + sl * 65 + 65],
                            EB[:, eoff + q0:eoff + 512],
                            start=(kc == 0), stop=(kc == nkc - 1))
                    s.op("tensor", mmpv, reads=[bVB, ebuf],
                         writes=[bP[pvbank]])

            for kc in range(nkc):
                emit_s(kc)
                if kc >= 1:
                    emit_epv(kc - 1)
                if kc == 1 and pend:
                    for fn in pend:
                        fn()
                    pend = []
            emit_epv(nkc - 1)
            def tail_head(qb=qb, pvbase=pvbase):
                s.op("vector", lambda o: o.tensor_copy(
                    DRS[64:65, 0:512], ps[pvbase][64:65, :]),
                    reads=[bP[pvbase]], writes=[bDRS])
                s.op("vector", lambda o: o.tensor_copy(
                    DRS[64:65, 512:1024], ps[pvbase + 1][64:65, :]),
                    reads=[bP[pvbase + 1]], writes=[bDRS])
                s.op("scalar", lambda o: o.activation(
                    DRS[64:65, 1024:2048], DRS[64:65, 0:1024],
                    AF.Ln), reads=[bDRS], writes=[bDRS])
                s.op("scalar", lambda o: o.activation(
                    DRS[64:65, 0:1024], DRS[64:65, 1024:2048],
                    AF.Exp, scale=-1.0), reads=[bDRS], writes=[bDRS])
                for sl in range(2):
                    src = DRS[64:65, sl * 512:sl * 512 + 512].rearrange(
                        "p (u n) -> p u n", u=1).broadcast_to([1, 64, 512])
                    s.op("gpsimd", lambda o, sl=sl, src=src: o.dma_start(
                        out=XT[0:64, 1024 + sl * 512:1024 + sl * 512 + 512],
                        in_=src), reads=[bDRS], writes=[bXT[1]])

            def tail_apply(qb=qb, pvbase=pvbase, hA=hA, hB=hB, high=high):
                for sl, hh in ((0, hA), (1, hB)):
                    hlow = hh % 8
                    if not high:
                        s.op("vector", lambda o, sl=sl, hlow=hlow,
                             pvbank=pvbase + sl, qb=qb: o.tensor_mul(
                                 A[0:64, hlow * 2048 + qb * 512:
                                   hlow * 2048 + qb * 512 + 512],
                                 ps[pvbank][0:64, :],
                                 XT[0:64, 1024 + sl * 512:
                                    1024 + sl * 512 + 512]),
                             reads=[bP[pvbase + sl], bXT[1]],
                             writes=list(bAs))
                    else:
                        s.op("vector", lambda o, sl=sl,
                             pvbank=pvbase + sl: o.tensor_mul(
                                 PS1[0:64, sl * 512:sl * 512 + 512],
                                 ps[pvbank][0:64, :],
                                 XT[0:64, 1024 + sl * 512:
                                    1024 + sl * 512 + 512]),
                             reads=[bP[pvbase + sl], bXT[1]],
                             writes=[bPS1h[0]])
                        s.op("gpsimd", lambda o, sl=sl, hlow=hlow, qb=qb:
                             o.dma_start(
                                 out=A[64:128, hlow * 2048 + qb * 512:
                                       hlow * 2048 + qb * 512 + 512],
                                 in_=PS1[0:64, sl * 512:sl * 512 + 512]),
                             reads=[bPS1h[0]], writes=list(bAs))
            if qb < 3:
                pend.append(lambda th=tail_head, ta=tail_apply:
                            (th(), ta()))
            else:
                tail_head()
                pend.append(tail_apply)
    for fn in pend:
        fn()
    pend = []

    oT_of = lambda kc, nb: A[:, kc * 2048 + nb * 512:kc * 2048 + nb * 512 + 512]
    for m in range(8):
        slot = m % 2
        if m >= 2:
            s.op("sync", dma(WS[:, slot * 1024:slot * 1024 + 1024],
                             d_wout[:, m * 1024:(m + 1) * 1024]),
                 writes=[bWS[slot]])
        proj_mblock(lambda nb, m=m: B[:, m * 2048 + nb * 512:
                                      m * 2048 + nb * 512 + 512],
                    lambda nb: bBs[4 * nb:4 * nb + 4], slot,
                    lambda nb: list(bAs), oT_of, 8)

    def ln_stats_ps(bufs, b0, b1, sl):
        mo, no = sl * 4, (sl % 4) * 12
        s.op("vector", lambda o: (
            o.bn_stats(NS[:, no:no + 6], b0),
            o.bn_stats(NS[:, no + 6:no + 12], b1))[-1],
            reads=bufs, writes=[bNS[sl % 4]])
        s.op("vector", lambda o: o.bn_aggr(MV[:, mo:mo + 2],
                                           NS[:, no:no + 12]),
             reads=[bNS[sl % 4]], writes=[bMV[sl]])
        s.op("scalar", lambda o: o.activation(MV[:, mo + 3:mo + 4],
                                              MV[:, mo + 1:mo + 2], AF.Ln,
                                              bias=EPSC[:, 0:1], scale=1.0),
             reads=[bMV[sl], bCONST], writes=[bMV[sl]])
        s.op("scalar", lambda o: o.activation(MV[:, mo + 2:mo + 3],
                                              MV[:, mo + 3:mo + 4], AF.Exp,
                                              scale=-0.5),
             reads=[bMV[sl]], writes=[bMV[sl]])

    def p5_a(t):
        par = t % 2
        s.op("gpsimd", dma(XT[:, par * 1024:par * 1024 + 1024],
                           d_x[t * 128:(t + 1) * 128, :]), writes=[bXT[par]])
        for hf in range(2):
            bank = par * 2 + hf

            def tra(o, hf=hf, t=t, bank=bank):
                for j in range(4):
                    r = o.transpose(ps[bank][:, j * 128:(j + 1) * 128],
                                    B[:, (hf * 4 + j) * 2048 + t * 128:
                                      (hf * 4 + j) * 2048 + t * 128 + 128],
                                    IDT[:])
                return r
            s.op("tensor", tra, reads=[bBs[t], bCONST], writes=[bP[bank]])
        ln_stats_ps([bP[par * 2], bP[par * 2 + 1]],
                    ps[par * 2][:], ps[par * 2 + 1][:], par)

    def p5_b(t):
        par = t % 2
        mo = par * 4
        eoff = par * 1024
        for hf in range(2):
            bank = par * 2 + hf
            s.op("vector", lambda o, mo=mo, eoff=eoff, hf=hf, bank=bank:
                 o.tensor_scalar(
                     EB[:, eoff + hf * 512:eoff + hf * 512 + 512],
                     ps[bank][:], MV[:, mo:mo + 1], MV[:, mo + 2:mo + 3],
                     ALU.subtract, ALU.mult),
                 reads=[bP[bank], bMV[par]],
                 writes=[bEB[2 * par + hf]])
        s.op("vector", lambda o, t=t, par=par, eoff=eoff: o.tensor_add(
            A[:, t * 1024:(t + 1) * 1024], EB[:, eoff:eoff + 1024],
            XT[:, par * 1024:par * 1024 + 1024]),
            reads=[bEB[2 * par], bEB[2 * par + 1], bXT[par]],
            writes=[bAs[t]])
        ln_stats(bAs[t], A[:, t * 1024:(t + 1) * 1024], 2 + par)

    def p5_c(t):
        par = t % 2
        mo = (2 + par) * 4
        s.op("vector", lambda o, t=t, mo=mo, par=par: o.tensor_scalar(
            EB[:, par * 1024:par * 1024 + 1024],
            A[:, t * 1024:(t + 1) * 1024], MV[:, mo:mo + 1],
            MV[:, mo + 2:mo + 3], ALU.subtract, ALU.mult),
            reads=[bAs[t], bMV[2 + par]],
            writes=[bEB[2 * par], bEB[2 * par + 1]]),
        tr_tile_to(lambda f, tt: B[:, f * 2048 + tt * 128:
                                   f * 2048 + tt * 128 + 128],
                   t, bEB[2 * par], EB, (par * 2, par * 2 + 1), [bBs[t]],
                   srcoff=par * 1024, dstbuf_is_list=True)

    def gate_mm(nb):
        def mmgate(o, nb=nb):
            for kc in range(8):
                r = o.matmul(ps[7][0:NE, :], WU[:, kc * 8:(kc + 1) * 8],
                             ln3_of(kc, nb), start=(kc == 0), stop=(kc == 7))
            return r
        s.op("tensor", mmgate, reads=[bWU[0]] + bBs[4 * nb:4 * nb + 4],
             writes=[bP[7]])
        s.op("vector", lambda o, nb=nb: o.tensor_scalar_add(
            LG[0:NE, nb * 512:(nb + 1) * 512], ps[7][0:NE, :], BG[0:NE, 0:1]),
            reads=[bP[7], bCONST], writes=[bLG])

    s.op("sync", dma(WU[:, 0:64], d_wg[:, :]), writes=[bWU[0]])
    ln3_of = lambda kc, nb: B[:, kc * 2048 + nb * 512:
                              kc * 2048 + nb * 512 + 512]
    def gate_soft(nb):
        for tt in range(4):
            t = nb * 4 + tt
            s.op("tensor", lambda o, t=t: o.transpose(
                ps[4][:, 0:NE], LG[0:NE, t * 128:(t + 1) * 128],
                IDT[0:NE, 0:NE]), reads=[bLG, bCONST], writes=[bP[4]])
            s.op("vector", lambda o, t=t: o.tensor_copy(
                LGT[:, t * NE:(t + 1) * NE], ps[4][:, 0:NE]),
                reads=[bP[4]], writes=[bLGT])
            s.op("vector", lambda o, t=t: o.reduce_max(
                GSS[:, 16 + t:17 + t], LGT[:, t * NE:(t + 1) * NE],
                axis=AX.X), reads=[bLGT], writes=[bGSS])
            s.op("vector", lambda o, t=t: o.tensor_scalar_sub(
                GES[:, t * NE:(t + 1) * NE], LGT[:, t * NE:(t + 1) * NE],
                GSS[:, 16 + t:17 + t]), reads=[bLGT, bGSS], writes=[bGES])
        s.op("scalar", lambda o, nb=nb: o.activation(
            LGT[:, nb * 32:(nb + 1) * 32], GES[:, nb * 32:(nb + 1) * 32],
            AF.Exp), reads=[bGES], writes=[bLGT])
        for tt in range(4):
            t = nb * 4 + tt
            s.op("vector", lambda o, t=t: o.reduce_sum(
                GSS[:, t:t + 1], LGT[:, t * NE:(t + 1) * NE], axis=AX.X),
                reads=[bLGT], writes=[bGSS])
        s.op("scalar", lambda o, nb=nb: o.activation(
            GSS[:, 16 + 4 * nb:16 + 4 * nb + 4],
            GSS[:, 4 * nb:4 * nb + 4], AF.Ln),
            reads=[bGSS], writes=[bGSS])
        s.op("scalar", lambda o, nb=nb: o.activation(
            GSS[:, 4 * nb:4 * nb + 4], GSS[:, 16 + 4 * nb:16 + 4 * nb + 4],
            AF.Exp, scale=-1.0), reads=[bGSS], writes=[bGSS])
        for tt in range(4):
            t = nb * 4 + tt
            s.op("tensor", lambda o, t=t: o.transpose(
                ps[4][0:1, 0:128], GSS[:, t:t + 1], IDT[:]),
                reads=[bGSS, bCONST], writes=[bP[4]])
            s.op("vector", lambda o, t=t: o.tensor_copy(
                BPROW[0:1, t * 128:(t + 1) * 128], ps[4][0:1, 0:128]),
                reads=[bP[4]], writes=[bBPROW])
        s.op("tensor", lambda o, nb=nb: o.matmul(
            ps[4][:], ONE[0:1, :], BPROW[0:1, nb * 512:(nb + 1) * 512],
            start=True, stop=True), reads=[bBPROW, bCONST], writes=[bP[4]])
        s.op("vector", lambda o, nb=nb: o.tensor_copy(
            KP[:, nb * 512:(nb + 1) * 512], ps[4][:]),
            reads=[bP[4]], writes=[bKP])

    p6q = []

    def enqueue_strip(s4):
        for mf in range(8):
            def blk(mf=mf, s4=s4):
                slot = mf % 2
                s.op("sync", dma(WS[:, slot * 1024:slot * 1024 + 1024],
                                 d_we[:, mf * 1024:(mf + 1) * 1024]),
                     writes=[bWS[slot]])
                bank = 4 + mf % 4

                def mme(o, mf=mf, s4=s4, slot=slot, bank=bank):
                    for kc in range(8):
                        r = o.matmul(ps[bank][:],
                                     WS[:, slot * 1024 + kc * 128:
                                        slot * 1024 + kc * 128 + 128],
                                     ln3_of(kc, s4), start=(kc == 0),
                                     stop=(kc == 7))
                    return r
                s.op("tensor", mme,
                     reads=[bWS[slot]] + bBs[4 * s4:4 * s4 + 4],
                     writes=[bP[bank]])
                s.op("vector", lambda o, mf=mf, s4=s4, bank=bank:
                     o.scalar_tensor_tensor(
                         C[:, mf * 512:(mf + 1) * 512], ps[bank][:],
                         BE[:, mf:mf + 1], KP[:, s4 * 512:(s4 + 1) * 512],
                         ALU.add, ALU.mult),
                     reads=[bP[bank], bCONST, bKP], writes=[bC])
            p6q.append(blk)

        def p6_a(tt, s4=s4):
            t = s4 * 4 + tt
            par = t % 2
            for hf in range(2):
                bank = 4 + par * 2 + hf

                def trf(o, hf=hf, tt=tt, bank=bank):
                    for j in range(4):
                        f = hf * 4 + j
                        r = o.transpose(ps[bank][:, j * 128:(j + 1) * 128],
                                        C[:, f * 512 + tt * 128:
                                          f * 512 + tt * 128 + 128], IDT[:])
                    return r
                s.op("tensor", trf, reads=[bC, bCONST], writes=[bP[bank]])
            ln_stats_ps([bP[4 + par * 2], bP[5 + par * 2]],
                        ps[4 + par * 2][:], ps[5 + par * 2][:], 4 + par)

        def p6_b(tt, s4=s4):
            t = s4 * 4 + tt
            par = t % 2
            mo = (4 + par) * 4
            if DBG and t == DBG_T:
                s.op("sync", lambda o, t=t: o.dma_start(
                    out=d_dbg[:, 2048:3072],
                    in_=A[:, t * 1024:(t + 1) * 1024]),
                    reads=[bAs[t]], writes=[bDBG])
                s.op("sync", lambda o, mo=mo: o.dma_start(
                    out=d_dbg[:, 3072:3076], in_=MV[:, mo:mo + 4]),
                    reads=[bMV[4 + par]], writes=[bDBG])
            for hf in range(2):
                bank = 4 + par * 2 + hf
                s.op("vector", lambda o, mo=mo, par=par, hf=hf, bank=bank:
                     o.tensor_scalar(
                         EB[:, par * 1024 + hf * 512:
                            par * 1024 + hf * 512 + 512],
                         ps[bank][:], MV[:, mo:mo + 1], MV[:, mo + 2:mo + 3],
                         ALU.subtract, ALU.mult),
                     reads=[bP[bank], bMV[4 + par]],
                     writes=[bEB[2 * par + hf]])
            if DBG and t == DBG_T:
                s.op("sync", lambda o, par=par: o.dma_start(
                    out=d_dbg[:, 1024:2048],
                    in_=EB[:, par * 1024:par * 1024 + 1024]),
                    reads=[bEB[2 * par], bEB[2 * par + 1]], writes=[bDBG])
            s.op("vector", lambda o, t=t, par=par: o.tensor_add(
                PS1[:, par * 1024:par * 1024 + 1024],
                EB[:, par * 1024:par * 1024 + 1024],
                A[:, t * 1024:(t + 1) * 1024]),
                reads=[bEB[2 * par], bEB[2 * par + 1], bAs[t]],
                writes=[bPS1h[par]])
            s.op("gpsimd", lambda o, t=t, par=par: o.dma_start(
                out=d_y[t * 128:(t + 1) * 128, :],
                in_=PS1[:, par * 1024:par * 1024 + 1024]),
                reads=[bPS1h[par]], writes=[bY])

        p6q.append(lambda: p6_a(0))
        p6q.append(lambda: p6_a(1))
        p6q.append(lambda: p6_b(0))
        p6q.append(lambda: p6_a(2))
        p6q.append(lambda: p6_b(1))
        p6q.append(lambda: p6_a(3))
        p6q.append(lambda: p6_b(2))
        p6q.append(lambda: p6_b(3))

    p5_a(0)
    p5_a(1)
    p5_b(0)
    for t in range(NT):
        if t + 1 < NT:
            p5_b(t + 1)
        p5_c(t)
        if t + 2 < NT:
            p5_a(t + 2)
        if t % 4 == 3:
            p6q.append(lambda s4=t // 4: gate_mm(s4))
            p6q.append(lambda s4=t // 4: gate_soft(s4))
            enqueue_strip(t // 4)
        for _ in range(5):
            if p6q:
                p6q.pop(0)()
    while p6q:
        p6q.pop(0)()

    for i in range(NDMASEM):
        key = f"gpsimd{i}"
        fv = s.count[key]
        if fv:
            s.raw("gpsimd", lambda o, key=key, fv=fv: o.wait_ge(
                s.sems[key], fv))

    block = es.enter_context(nc.Block())
    with es:
        for ename in ENGINES:
            getattr(block, ename)(
                (lambda prog: lambda obj: [f(obj) for f in prog])(
                    s.prog[ename]))
    return nc


_NC_CACHE = []


def prepare_inputs(L0_attn_x_bld, L0_attn_Wq, L0_attn_Wkv_down,
                   L0_attn_Wk_up, L0_attn_Wv_up, L0_attn_Wout, L0_ffn_Wg,
                   L0_ffn_bg, L0_ffn_We, L0_ffn_be):
    def streamed(w, nm):
        k = w.shape[0]
        return np.ascontiguousarray(
            w.reshape(k // 128, 128, nm, 128).transpose(1, 2, 0, 3)
            .reshape(128, nm * (k // 128) * 128))

    x = np.ascontiguousarray(np.asarray(L0_attn_x_bld, dtype=np.float32))
    wq = streamed(np.asarray(L0_attn_Wq, dtype=np.float32).reshape(DM, DM), 8)
    wkv = streamed(np.asarray(L0_attn_Wkv_down, dtype=np.float32), 2)
    wku = streamed(np.asarray(L0_attn_Wk_up,
                              dtype=np.float32).reshape(LD, DM), 8)
    wvu = streamed(np.asarray(L0_attn_Wv_up,
                              dtype=np.float32).reshape(LD, DM), 8)
    wout0 = np.asarray(L0_attn_Wout, dtype=np.float32).reshape(DM, DM)
    wout = streamed(np.ascontiguousarray(np.concatenate(
        [np.concatenate([wout0[h * 64:(h + 1) * 64],
                         wout0[(h + 8) * 64:(h + 9) * 64]], axis=0)
         for h in range(8)], axis=0)), 8)
    wg = np.ascontiguousarray(
        np.asarray(L0_ffn_Wg, dtype=np.float32).reshape(8, 128, NE)
        .transpose(1, 0, 2).reshape(128, 64))
    bgc = np.ascontiguousarray(np.asarray(L0_ffn_bg,
                                          dtype=np.float32).reshape(NE, 1))
    we = streamed(np.asarray(L0_ffn_We, dtype=np.float32), 8)
    bet = np.ascontiguousarray(np.asarray(L0_ffn_be,
                                          dtype=np.float32).reshape(NFC,
                                                                    128).T)
    cid = np.eye(128, dtype=np.float32)
    cones = np.ones((128, 128), dtype=np.float32)
    ctri = np.triu(np.ones((128, 128), dtype=np.float32))
    cseg = np.zeros((128, 2), dtype=np.float32)
    cseg[0:64, 0] = 1.0
    cseg[64:128, 1] = 1.0
    import ml_dtypes
    csegs = np.zeros((128, 128), dtype=ml_dtypes.bfloat16)
    for m in range(8):
        csegs[0:64, m * 16 + 2 * m] = 1.0
        csegs[64:128, m * 16 + 2 * m + 1] = 1.0
    csegb = np.ascontiguousarray(cseg.astype(ml_dtypes.bfloat16))

    shared = {"wq": wq, "wkv": wkv, "wku": wku, "wvu": wvu, "wout": wout,
              "wg": wg, "bgc": bgc, "we": we, "bet": bet, "cid": cid,
              "cones": cones, "ctri": ctri, "cseg": cseg, "csegs": csegs, "csegb": csegb}
    return shared, x


def kernel(**inputs):
    shared, x = prepare_inputs(**inputs)
    if not _NC_CACHE:
        _NC_CACHE.append(build_nc())
    nc = _NC_CACHE[0]
    in_maps = [dict(shared, x=np.ascontiguousarray(x[b])) for b in range(8)]
    res = run_bass_kernel_spmd(nc, in_maps, core_ids=list(range(8)))
    out = np.stack([np.asarray(r["y"], dtype=np.float32)
                    for r in res.results], axis=0)
    return out
